# version 10
# baseline (speedup 1.0000x reference)
import numpy as np
import ml_dtypes
from contextlib import ExitStack

import concourse.bass as bass
import concourse.mybir as mybir
import concourse.bass_utils as _bass_utils
from concourse.bass_utils import run_bass_kernel_spmd

if not getattr(_bass_utils, "_ldwopt_patched", False):
    _orig_run_command = _bass_utils.run_command

    def _run_command_ldwopt(cmd, **kw):
        cmd = ["--enable-ldw-opt=true" if c == "--enable-ldw-opt=false" else c
               for c in cmd]
        return _orig_run_command(cmd, **kw)

    _bass_utils.run_command = _run_command_ldwopt
    _bass_utils._ldwopt_patched = True

F32 = mybir.dt.float32
BF16 = mybir.dt.bfloat16
F8 = mybir.dt.float8e4
I32 = mybir.dt.int32
ALU = mybir.AluOpType
ACTF = mybir.ActivationFunctionType
AX = mybir.AxisListType

T = 2048
D = 2048
H = 8192
O1 = 2 * H
O2 = D
NT = T // 128
ND = D // 128
NH = H // 128
NWP = H // 512
EPS = 1e-5
QB = 127.0
INV_D = 1.0 / D
N_WGV = float(O1 * D)
N_WOUT = float(O2 * H)

ENGS = ("T", "V", "S", "Q", "G")


NQSEM = 8
QUEUES = ("Q", "G", "R")


class _Op:
    __slots__ = ("eng", "emit", "waits", "dma", "ndma", "need_inc", "sem", "val",
                 "prev_val", "hard")

    def __init__(self, eng, emit, waits, dma, ndma):
        self.eng, self.emit, self.waits, self.dma = eng, emit, list(waits), dma
        self.ndma = ndma
        self.need_inc = dma
        self.sem = None
        self.val = None
        self.prev_val = 0


class Sched:

    def __init__(self, prev=None):
        self.ops = []
        self.prev = prev

    def add(self, eng, emit, waits=(), dma=False, ndma=1, hard_waits=()):
        op = _Op(eng, emit, [w for w in waits if w is not None], dma, ndma)
        op.hard = [w for w in hard_waits if w is not None]
        self.ops.append(op)
        return op

    def finalize(self):
        prev = {"V": None, "S": None, "G": None}
        for op in self.ops:
            if op.eng in prev:
                if prev[op.eng] is not None:
                    op.hard.append(prev[op.eng])
                prev[op.eng] = op
        lastc = {}
        for op in self.ops:
            if not op.dma and op.eng in ("T", "V", "S", "G"):
                lastc[op.eng] = op
        for op in lastc.values():
            op.need_inc = True
        for op in self.ops:
            for w in op.waits:
                if w.eng != op.eng or w.dma:
                    w.need_inc = True
            for w in op.hard:
                w.need_inc = True
        if self.prev is not None:
            cnt = dict(self.prev.final_cnt)
            qrot = dict(self.prev.final_qrot)
            qcnt = dict(self.prev.qtot)
            self.entry_waits = dict(cnt)
            for k, v in qcnt.items():
                self.entry_waits[k] = v
        else:
            cnt = {e: 0 for e in ("T", "V", "S", "G")}
            qrot = {q: 0 for q in QUEUES}
            qcnt = {(q, i): 0 for q in QUEUES for i in range(NQSEM)}
            self.entry_waits = {}
        self.qtot = qcnt
        for op in self.ops:
            if op.dma:
                i = qrot[op.eng]
                qrot[op.eng] = (i + 1) % NQSEM
                op.sem = (op.eng, i)
                op.prev_val = qcnt[op.sem]
                qcnt[op.sem] += 16 * op.ndma
                op.val = qcnt[op.sem]
            elif op.need_inc:
                cnt[op.eng] += 1
                op.sem = op.eng
                op.val = cnt[op.eng]
        for op in self.ops:
            ws = {}
            for w in op.waits:
                if not w.need_inc:
                    continue
                if (not w.dma) and w.eng == op.eng:
                    continue
                ws[w.sem] = max(ws.get(w.sem, 0), w.val)
            for w in op.hard:
                ws[w.sem] = max(ws.get(w.sem, 0), w.val)
            op.waits = ws
        self.final_cnt = cnt
        self.final_qrot = qrot

    def replay(self, eng_name, eng, sems):
        names = ("S", "R") if eng_name == "S" else (eng_name,)
        last = {}
        for sk, v in self.entry_waits.items():
            if v > 0 and sk not in names:
                eng.wait_ge(sems[sk], v)
                last[sk] = v
            else:
                last[sk] = v
        for op in self.ops:
            if op.eng not in names:
                continue
            if op.dma and op.prev_val > 0 and last.get(op.sem, 0) < op.prev_val:
                eng.wait_ge(sems[op.sem], op.prev_val)
                last[op.sem] = op.prev_val
            for sk, v in op.waits.items():
                if v > last.get(sk, 0):
                    eng.wait_ge(sems[sk], v)
                    last[sk] = v
            out = op.emit(eng)
            if op.dma:
                insts = out if isinstance(out, list) else [out]
                assert len(insts) == op.ndma, (op.eng, len(insts), op.ndma)
                for inst in insts:
                    inst.then_inc(sems[op.sem], 16)
            elif op.need_inc:
                out.then_inc(sems[op.sem], 1)
        for qn in names:
            if qn in QUEUES:
                for i in range(NQSEM):
                    tot = self.qtot[(qn, i)]
                    if tot > last.get((qn, i), 0):
                        eng.wait_ge(sems[(qn, i)], tot)


def _seq(sch, eng, emits, waits=()):
    t = None
    for i, em in enumerate(emits):
        t = sch.add(eng, em, waits if i == 0 else ())
    return t


def _wire(nc, sched, sems):
    sched.finalize()
    with nc.Block() as blk:
        blk.tensor(lambda e: sched.replay("T", e, sems))
        blk.vector(lambda e: sched.replay("V", e, sems))
        blk.scalar(lambda e: sched.replay("S", e, sems))
        blk.sync(lambda e: sched.replay("Q", e, sems))
        blk.gpsimd(lambda e: sched.replay("G", e, sems))


def build_nc():
    nc = bass.Bass("TRN2", target_bir_lowering=False)

    d_x = nc.dram_tensor("x", [T, D], F32, kind="ExternalInput").ap()
    d_xT = nc.dram_tensor("xT", [D, T], F32, kind="ExternalInput").ap()
    d_wgvT = nc.dram_tensor("wgvT", [D, O1], F32, kind="ExternalInput").ap()
    d_woutT = nc.dram_tensor("woutT", [H, O2], F32, kind="ExternalInput").ap()
    d_wvT = nc.dram_tensor("wvT", [D, 128], F32, kind="ExternalInput").ap()
    d_bv = nc.dram_tensor("bv", [1, 128], F32, kind="ExternalInput").ap()
    d_nwb = nc.dram_tensor("nwb", [128, D], F32, kind="ExternalInput").ap()
    d_idf = nc.dram_tensor("idf", [128, 128], F32, kind="ExternalInput").ap()
    d_idb = nc.dram_tensor("idb", [128, 128], BF16, kind="ExternalInput").ap()
    d_ones1 = nc.dram_tensor("ones1", [1, 128], F32, kind="ExternalInput").ap()
    d_ones128 = nc.dram_tensor("ones128", [128, 1], F32, kind="ExternalInput").ap()
    d_epsv = nc.dram_tensor("epsv", [128, 1], F32, kind="ExternalInput").ap()
    d_out = nc.dram_tensor("out", [T, D], F32, kind="ExternalOutput").ap()
    d_h2 = nc.dram_tensor("h2sp", [T, H], F32, kind="Internal").ap()
    d_x1 = nc.dram_tensor("x1sp", [T, D], F32, kind="Internal").ap()
    d_wq8 = nc.dram_tensor("wq8sp", [H, O2], F8, kind="Internal").ap()
    d_sc1 = nc.dram_tensor("sc1sp", [1, 1], F32, kind="Internal").ap()

    es = ExitStack()
    with es:
        c_idb = es.enter_context(nc.sbuf_tensor("c_idb", [128, 128], BF16))
        c_ones1 = es.enter_context(nc.sbuf_tensor("c_ones1", [1, 128], F32))
        c_ones128 = es.enter_context(nc.sbuf_tensor("c_ones128", [128, 1], F32))
        c_bv = es.enter_context(nc.sbuf_tensor("c_bv", [1, 128], F32))
        c_eps = es.enter_context(nc.sbuf_tensor("c_eps", [128, 1], F32))
        v_Mh = es.enter_context(nc.sbuf_tensor("v_Mh", [128, NT], F32))
        v_c1 = es.enter_context(nc.sbuf_tensor("v_c1", [128, NT], F32))
        v_hmax = es.enter_context(nc.sbuf_tensor("v_hmax", [128, NT], F32))
        v_c2 = es.enter_context(nc.sbuf_tensor("v_c2", [128, NT], F32))
        v_c3 = es.enter_context(nc.sbuf_tensor("v_c3", [128, NT], F32))
        v_sgv = es.enter_context(nc.sbuf_tensor("v_sgv", [128, 1], F32))
        v_hsgv = es.enter_context(nc.sbuf_tensor("v_hsgv", [128, 1], F32))
        v_sout = es.enter_context(nc.sbuf_tensor("v_sout", [128, 1], F32))
        v_hsout = es.enter_context(nc.sbuf_tensor("v_hsout", [128, 1], F32))
        v_nhsout = es.enter_context(nc.sbuf_tensor("v_nhsout", [128, 1], F32))
        v_ti = es.enter_context(nc.sbuf_tensor("v_ti", [128, 8], F32))


        sems = {e: es.enter_context(nc.semaphore(f"sem_{e}")) for e in ("T", "V", "S", "G")}
        for qe in QUEUES:
            for i in range(NQSEM):
                sems[(qe, i)] = es.enter_context(nc.semaphore(f"sem_{qe}{i}"))

        esAB = ExitStack()
        with esAB:
            k1T = esAB.enter_context(nc.sbuf_tensor("k1T", [128, ND * T], BF16))
            b_wo8 = [esAB.enter_context(nc.sbuf_tensor(f"b_wo8{i}", [128, 2048], F8)) for i in range(1)]


            esA = ExitStack()
            with esA:
                c_wvT = esA.enter_context(nc.sbuf_tensor("c_wvT", [128, ND * 128], F32))
                c_nwb = esA.enter_context(nc.sbuf_tensor("c_nwb", [128, D], F32))
                a_x = [esA.enter_context(nc.sbuf_tensor(f"a_x{i}", [128, D], F32)) for i in range(3)]
                a_xT = esA.enter_context(nc.sbuf_tensor("a_xT", [128, ND * 128], F32))
                a_z = esA.enter_context(nc.sbuf_tensor("a_z", [128, D], F32))
                a_u = esA.enter_context(nc.sbuf_tensor("a_u", [128, D], F32))
                a_kt = esA.enter_context(nc.sbuf_tensor("a_kt", [128, D], I32))
                a_m1 = esA.enter_context(nc.sbuf_tensor("a_m1", [128, D], F32))
                a_m2 = esA.enter_context(nc.sbuf_tensor("a_m2", [128, D], F32))
                a_kb = esA.enter_context(nc.sbuf_tensor("a_kb", [128, D], BF16))
                a_gvB = esA.enter_context(nc.sbuf_tensor("a_gvB", [128, 4 * D], F32))
                a_y2 = esA.enter_context(nc.sbuf_tensor("a_y2", [128, D], F32))
                a_pgv = esA.enter_context(nc.sbuf_tensor("a_pgv", [128, 64], F32))
                st = esA.enter_context(nc.sbuf_tensor("a_st", [128, 16], F32))
                ps_v = esA.enter_context(nc.psum_tensor("ps_v", [128, 512], F32))
                ps_s = esA.enter_context(nc.psum_tensor("ps_s", [128, 512], F32))
                ps_tr = [esA.enter_context(nc.psum_tensor(f"ps_tr{i}", [128, 512], F32)) for i in range(2)]
                ps_trb = [esA.enter_context(nc.psum_tensor(f"ps_trbA{i}", [128, 1024], BF16)) for i in range(2)]

                sch = Sched()
                nc._schA = sch

                t_pre_all = []
                for dst, src in [
                    (c_idb[:, :], d_idb),
                    (c_ones1[:, :], d_ones1), (c_ones128[:, :], d_ones128),
                    (c_bv[:, :], d_bv), (c_nwb[:, :], d_nwb),
                    (c_eps[:, :], d_epsv),
                    (c_wvT[:, :].rearrange("p (a j) -> p a j", j=128),
                     d_wvT.rearrange("(a p) j -> p a j", p=128)),
                ]:
                    t_pre_all.append(sch.add("Q", (lambda dd, ss: (lambda e: e.dma_start(dd, ss)))(dst, src), dma=True))
                t_pre = t_pre_all[-1]

                NSG = 128
                t_gvl = [[None, None] for _ in range(NSG // 2)]
                t_gvp = [None] * (NSG // 2)
                t_x = [None] * NT
                t_z = [None] * NT
                t_u = [None] * NT
                t_ysp = [None] * NT
                t_qk = [None] * NT
                t_kb = [None] * NT
                t_k1e = [[None, None] for _ in range(NT)]
                t_xTl = [None] * NT
                t_vmmA = [None] * NT
                t_ktr = [[None, None] for _ in range(NT)]
                t_cact = [None] * NT

                def emit_sload(p):
                    for i, t in enumerate((2 * p, 2 * p + 1)):
                        rb, cs = t // 8, t % 8
                        s = t % 4
                        w = [t_gvp[(t - 4) // 2]] if t >= 4 else list(t_pre_all)
                        qn = ("Q", "R", "G")[t % 3]
                        srcp = d_wgvT[rb * 128:(rb + 1) * 128, cs * 2048:(cs + 1) * 2048]
                        t_gvl[p][i] = sch.add(qn, (lambda s, ss: (lambda e: e.dma_start(
                            a_gvB[:, s * 2048:(s + 1) * 2048], ss)))(s, srcp),
                            w, dma=True)

                def emit_sabs(p):
                    s0 = (2 * p) % 4
                    t_gvp[p] = sch.add("S", (lambda s0=s0, p=p: (lambda e: e.activation(
                        a_gvB[:, s0 * 2048:s0 * 2048 + 4096],
                        a_gvB[:, s0 * 2048:s0 * 2048 + 4096], ACTF.Abs,
                        accum_out=a_pgv[:, p:p + 1])))(), list(t_gvl[p]))

                def emit_xload(tt):
                    w = [t_zsq[tt - 3]] if tt >= 3 else list(t_pre_all)
                    t_x[tt] = sch.add("Q", (lambda b, r0: (lambda e: e.dma_start(
                        a_x[b][:, :], d_x[r0:r0 + 128, :])))(tt % 3, tt * 128), w, dma=True)

                def emit_xT(tt):
                    w = [t_vmmA[tt - 1]] if tt >= 1 else list(t_pre_all)
                    t_xTl[tt] = sch.add("Q", (lambda r0=tt * 128: (lambda e: e.dma_start(
                        a_xT[:, :], d_xT[r0:r0 + 128, :])))(),
                        w, dma=True)

                def emit_vmm(tt):
                    def em_v(e):
                        for dt in range(ND):
                            e.matmul(ps_v[:, 0:128],
                                     a_xT[:, dt * 128:(dt + 1) * 128],
                                     c_wvT[:, dt * 128:(dt + 1) * 128],
                                     start=(dt == 0), stop=False)
                        return e.matmul(ps_v[:, 0:128], c_ones1[0:1, :], c_bv[0:1, :],
                                        start=False, stop=True)
                    w = [t_xTl[tt]] + ([t_z[tt - 1]] if tt >= 1 else [])
                    t_vmmA[tt] = sch.add("T", em_v, w)
                    return t_vmmA[tt]

                def emit_z(tt, t_vmm):
                    def em_z(e, b=tt % 3):
                        vbc = ps_v[:, 0:128].unsqueeze(1).broadcast_to((128, ND, 128))
                        return e.scalar_tensor_tensor(a_z[:, :], a_x[b][:, :], 1.0, vbc,
                                                      ALU.mult, ALU.add,
                                                      accum_out=st[:, 0:1])
                    w = [t_vmm, t_x[tt]] + ([t_zsq[tt - 1]] if tt >= 1 else [])
                    t_z[tt] = sch.add("V", em_z, w)
                    t_zsq[tt] = sch.add("S", (lambda tt=tt: lambda e: e.activation(
                        a_x[tt % 3][:, :], a_z[:, :], ACTF.Square,
                        accum_out=st[:, 1:2]))(), [t_z[tt]])

                def emit_var(tt):
                    t_var = _seq(sch, "V", [
                        lambda e: e.tensor_scalar(st[:, 4:5], st[:, 0:1], INV_D, None, ALU.mult),
                        lambda e: e.tensor_scalar(st[:, 2:3], st[:, 1:2], INV_D, None, ALU.mult),
                        lambda e: e.tensor_mul(st[:, 12:13], st[:, 4:5], st[:, 4:5]),
                        lambda e: e.tensor_sub(st[:, 2:3], st[:, 2:3], st[:, 12:13]),
                    ], [t_zsq[tt]])
                    t_sq1 = sch.add("S", lambda e: e.activation(
                        st[:, 3:4], st[:, 2:3], ACTF.Sqrt, bias=c_eps[:, 0:1]), [t_var])
                    t_y[tt] = sch.add("V", lambda e: e.reciprocal(
                        st[:, 8:9], st[:, 3:4]), [t_sq1])

                def emit_iter(tt):
                    nxt = tt + 1 if tt + 1 < NT else None
                    if nxt is not None:
                        emit_xT(nxt)
                        emit_xload(nxt)
                        emit_sload(nxt * 4 + 0)
                        emit_sload(nxt * 4 + 1)

                    w = [t_y[tt]] + ([t_qk[tt - 1]] if tt >= 1 else [])
                    t_p = _seq(sch, "V", [
                        lambda e: e.scalar_tensor_tensor(a_u[:, :], a_z[:, :],
                                                         st[:, 4:5], c_nwb[:, :],
                                                         ALU.subtract, ALU.mult),
                        lambda e: e.reduce_max(st[:, 10:11], a_u[:, :], axis=AX.X,
                                               apply_absolute_value=True),
                    ], w)
                    w = [t_ysp[tt - 1]] if tt >= 1 else []
                    t_y2 = _seq(sch, "V", [
                        lambda e: e.tensor_scalar(st[:, 5:6], st[:, 8:9], 2.0, None,
                                                  ALU.mult),
                        lambda e: e.tensor_scalar(a_y2[:, :], a_z[:, :], st[:, 4:5],
                                                  st[:, 5:6], ALU.subtract, ALU.mult),
                    ], w)
                    t_ysp[tt] = sch.add("G", (lambda r0: (lambda e: e.dma_start(
                        d_x1[r0:r0 + 128, :], a_y2[:, :])))(tt * 128),
                        [t_y2], dma=True)
                    t_c1 = _seq(sch, "V", [
                        lambda e: e.tensor_mul(st[:, 6:7], st[:, 8:9], st[:, 8:9]),
                        lambda e: e.tensor_mul(st[:, 6:7], st[:, 6:7], st[:, 2:3]),
                    ], [t_p])
                    t_sq2 = sch.add("S", lambda e: e.activation(
                        st[:, 7:8], st[:, 6:7], ACTF.Sqrt, bias=c_eps[:, 0:1]),
                        [t_c1])
                    t_cact[tt] = _seq(sch, "V", [
                        lambda e: e.reciprocal(st[:, 11:12], st[:, 7:8]),
                        lambda e: e.tensor_mul(st[:, 12:13], st[:, 8:9], st[:, 11:12]),
                        lambda e: e.tensor_mul(st[:, 14:15], st[:, 12:13], st[:, 10:11]),
                        (lambda tt=tt: lambda e: e.tensor_scalar(
                            v_Mh[:, tt:tt + 1], st[:, 14:15], EPS, None, ALU.max))(),
                        (lambda tt=tt: lambda e: e.reciprocal(
                            st[:, 13:14], v_Mh[:, tt:tt + 1]))(),
                        lambda e: e.tensor_scalar(st[:, 9:10], st[:, 13:14], QB,
                                                  st[:, 12:13], ALU.mult, ALU.mult),
                    ], [t_sq2])
                    w = [t_cact[tt]] + ([t_kb[tt - 1]] if tt >= 1 else [])
                    t_qk[tt] = sch.add("S", lambda e: e.activation(
                        a_kt[:, :], a_u[:, :], ACTF.Copy, scale=st[:, 9:10]), w)
                    if nxt is not None:
                        emit_sabs(nxt * 4 + 0)
                        emit_sabs(nxt * 4 + 1)
                        emit_sload(nxt * 4 + 2)
                        emit_sload(nxt * 4 + 3)
                        t_vmm = emit_vmm(nxt)
                        emit_z(nxt, t_vmm)
                    w = [t_qk[tt]] + ([t_ktr[tt - 1][1]] if tt >= 1 else [])
                    t_kb[tt] = _seq(sch, "V", [
                        lambda e: e.scalar_tensor_tensor(a_m2[:, :], a_u[:, :],
                                                         st[:, 9:10], a_kt[:, :],
                                                         ALU.mult, ALU.subtract),
                        lambda e: e.tensor_scalar(a_m1[:, :], a_m2[:, :], -1.0, 0.5,
                                                  ALU.mult, ALU.is_gt),
                        lambda e: e.scalar_tensor_tensor(a_m1[:, :], a_m2[:, :], 0.5,
                                                         a_m1[:, :], ALU.is_gt,
                                                         ALU.subtract),
                        lambda e: e.tensor_add(a_kb[:, :], a_kt[:, :], a_m1[:, :]),
                    ], w)
                    if nxt is not None:
                        emit_var(nxt)
                        emit_sabs(nxt * 4 + 2)
                        emit_sabs(nxt * 4 + 3)
                    for g in range(2):
                        def em_kt2(e, g=g):
                            for j in range(8):
                                dt = g * 8 + j
                                inst = e.transpose(ps_trb[g][:, j * 128:(j + 1) * 128],
                                                   a_kb[:, dt * 128:(dt + 1) * 128],
                                                   c_idb[:, :])
                            return inst
                        w = [t_kb[tt]]
                        if tt >= 1:
                            w.append(t_k1e[tt - 1][g])
                        t_ktr[tt][g] = sch.add("T", em_kt2, w)

                        def em_ke(e, g=g, tt=tt):
                            out_ap = bass.AP(k1T, (g * 8) * T + tt * 128,
                                             [[ND * T, 128], [T, 8], [1, 128]])
                            return e.tensor_copy(out_ap, ps_trb[g][:, :])
                        t_k1e[tt][g] = sch.add("V", em_ke, [t_ktr[tt][g]])

                t_zsq = [None] * NT
                t_y = [None] * NT
                emit_xT(0)
                emit_xload(0)
                emit_z(0, emit_vmm(0))
                emit_var(0)
                emit_sload(0)
                emit_sload(1)
                emit_sabs(0)
                emit_sabs(1)
                emit_sload(2)
                emit_sload(3)
                emit_sabs(2)
                emit_sabs(3)
                for tt in range(NT):
                    emit_iter(tt)

                t_r1 = sch.add("V", lambda e: e.reduce_sum(
                    v_ti[:, 0:1], a_pgv[:, :], axis=AX.X), [t_gvp[NSG // 2 - 1]])
                t_r2 = sch.add("T", lambda e: e.matmul(
                    ps_s[0:1, 0:1], v_ti[:, 0:1], c_ones128[:, 0:1],
                    start=True, stop=True), [t_r1, t_z[NT - 1]])
                t_r3 = sch.add("V", lambda e: e.tensor_copy(v_ti[0:1, 4:5], ps_s[0:1, 0:1]),
                               [t_r2])
                t_r4 = sch.add("T", lambda e: e.matmul(
                    ps_v[:, 0:1], c_ones1[0:1, :], v_ti[0:1, 4:5],
                    start=True, stop=True), [t_r3])

                _seq(sch, "V", [
                    lambda e: e.tensor_scalar(v_sgv[:, 0:1], ps_v[:, 0:1], 1.0 / N_WGV, EPS,
                                              ALU.mult, ALU.max),
                    lambda e: e.tensor_scalar(v_hsgv[:, 0:1], v_sgv[:, 0:1], 0.5, None, ALU.mult),
                    lambda e: e.tensor_scalar(v_c1[:, :], v_Mh[:, :], 1.0 / QB, v_sgv[:, 0:1],
                                              ALU.mult, ALU.mult),
                    lambda e: e.memset(v_hmax[:, :], 0.0),
                ], [t_r4, t_cact[NT - 1]])

                _wire(nc, sch, sems)

            esB = ExitStack()
            with esB:
                b_wq = [esB.enter_context(nc.sbuf_tensor(f"b_wq{i}", [128, 8 * 512], F32)) for i in range(2)]
                b_q8 = [esB.enter_context(nc.sbuf_tensor(f"b_q8{i}", [128, ND * 512], F8)) for i in range(8)]
                b_h2 = [esB.enter_context(nc.sbuf_tensor(f"b_h2{i}", [128, 512], F32)) for i in range(3)]
                b_sl = [esB.enter_context(nc.sbuf_tensor(f"b_sl{i}", [128, 512], F32)) for i in range(4)]
                b_wo = [esB.enter_context(nc.sbuf_tensor(f"b_wo{i}", [128, 2048], F32)) for i in range(3)]
                b_po = esB.enter_context(nc.sbuf_tensor("b_po", [128, 64], F32))
                b_wm = esB.enter_context(nc.sbuf_tensor("b_wm", [128, 2048], BF16))
                b_hm = [esB.enter_context(nc.sbuf_tensor(f"b_hm{i}", [128, 1], F32)) for i in range(2)]
                ps_a = [esB.enter_context(nc.psum_tensor(f"ps_a{i}", [128, 512], F32)) for i in range(2)]
                ps_b = [esB.enter_context(nc.psum_tensor(f"ps_b{i}", [128, 512], F32)) for i in range(2)]
                ps_c = [esB.enter_context(nc.psum_tensor(f"ps_c{i}", [128, 512], F32)) for i in range(2)]
                ps_d = [esB.enter_context(nc.psum_tensor(f"ps_d{i}", [128, 512], F32)) for i in range(2)]

                sch = Sched(prev=nc._schA)
                nc._schB = sch

                NP = NWP // 2
                t_wq = [[None] * 4 for _ in range(NP)]
                t_mmg = [[None] * NT for _ in range(NP)]
                t_h2vw = [[None, None] for _ in range(NP * NT)]
                t_h2sp = [[None, None] for _ in range(NP * NT)]
                t_woab = [None] * 64
                t_woq = [None] * 64
                t_woqo = [None] * 64
                t_sout = None

                wgvT_3d = d_wgvT.rearrange("(a p) o -> p a o", p=128)
                qctr = [0]
                t_qld = {}
                t_q8q = {}

                def emit_qload(qg):
                    P, rr = divmod(qg, 8)
                    h, qp = divmod(rr, 2)
                    w_idx = 2 * P + (h % 2)
                    col0 = (h // 2) * H + w_idx * 512
                    w = [t_q8q[qg - 2]] if qg >= 2 else []
                    t_qld[qg] = sch.add("Q", (lambda b=qg % 2, col0=col0, qp=qp: (
                        lambda e: e.dma_start(
                            b_wq[b][:, :].rearrange("p (a j) -> p a j", j=512),
                            wgvT_3d[:, qp * 8:(qp + 1) * 8, col0:col0 + 512])))(),
                        w, dma=True)

                def pair_quant_items(P):
                    items = []
                    for h in range(4):
                        qslot = (P % 2) * 4 + (h % 2) + 2 * (h // 2)
                        for qp in range(2):
                            qg = qctr[0]
                            qctr[0] += 1
                            for c2 in range(2):
                                items.append((qg, h, qslot, qp, c2, P))
                    return items

                def dispense_q(items):
                    for _ in range(2):
                        if not items:
                            return
                        qg, h, qslot, qp, c2, P = items.pop(0)
                        if c2 == 0:
                            if qg not in t_qld:
                                emit_qload(qg)
                            if qg + 1 < 8 * NP and (qg + 1) not in t_qld:
                                emit_qload(qg + 1)
                        qbuf = b_q8[qslot]
                        w = []
                        if c2 == 0:
                            w = [t_qld[qg]]
                            if P >= 2 and qp == 0:
                                w.append(t_mmg[P - 2][NT - 1])
                        sch.add("V", (lambda b=qg % 2, c2=c2: (lambda e: e.tensor_scalar(
                            b_wm[:, :], b_wq[b][:, c2 * 2048:(c2 + 1) * 2048],
                            -1.0, v_hsgv[:, 0:1], ALU.mult, ALU.is_gt)))(), w)
                        tok = sch.add("V", (lambda b=qg % 2, qbuf=qbuf, qp=qp, c2=c2: (
                            lambda e: e.scalar_tensor_tensor(
                                qbuf[:, qp * 4096 + c2 * 2048:qp * 4096 + (c2 + 1) * 2048],
                                b_wq[b][:, c2 * 2048:(c2 + 1) * 2048],
                                v_hsgv[:, 0:1], b_wm[:, :], ALU.is_gt, ALU.subtract)))(), [])
                        if c2 == 1:
                            t_q8q[qg] = tok
                            if qp == 1:
                                t_wq[P][h] = tok

                wo_queue = []
                wo_state = {"next_load": 0, "next_q": 0, "loads": {}}

                def _emit_wo_load(tw):
                    w = ([t_woq[tw - 3]] if tw >= 3 else []) + [t_sout]
                    wo_state["loads"][tw] = sch.add(
                        "R", (lambda r0, b: (lambda e: e.dma_start(
                            b_wo[b][:, :], d_woutT[r0:r0 + 128, :])))(tw * 128, tw % 3),
                        w, dma=True)

                def dispense_wo():
                    i = wo_state["next_q"]
                    if i >= 64 or t_sout is None:
                        return
                    while wo_state["next_load"] < min(i + 3, 64):
                        _emit_wo_load(wo_state["next_load"])
                        wo_state["next_load"] += 1
                    tw = i
                    wo_state["next_q"] += 1
                    w2 = [wo_state["loads"][tw]] + ([t_woqo[tw - 1]] if tw >= 1 else [])
                    sch.add("V", (lambda b=tw % 3: lambda e: e.tensor_scalar(
                        b_wm[:, :], b_wo[b][:, :], -1.0,
                        v_hsout[:, 0:1], ALU.mult, ALU.is_gt))(), w2)
                    t_woq[tw] = sch.add("V", (lambda b=tw % 3, b8=0: lambda e: e.scalar_tensor_tensor(
                        b_wo8[b8][:, :], b_wo[b][:, :],
                        v_hsout[:, 0:1], b_wm[:, :],
                        ALU.is_gt, ALU.subtract))(), [])
                    t_woqo[tw] = sch.add("G", (lambda r0, b: (lambda e: e.dma_start(
                        d_wq8[r0:r0 + 128, :], b_wo8[b][:, :])))(tw * 128, 0),
                        [t_woq[tw]], dma=True)

                wos_state = {"next_load": 0, "next_abs": 0, "loads": {}}

                def dispense_wos():
                    i = wos_state["next_abs"]
                    if i >= 64:
                        return
                    while wos_state["next_load"] < min(i + 3, 64):
                        tw = wos_state["next_load"]
                        w = [t_woab[tw - 3]] if tw >= 3 else []
                        wos_state["loads"][tw] = sch.add(
                            "R", (lambda r0, b: (lambda e: e.dma_start(
                                b_wo[b][:, :], d_woutT[r0:r0 + 128, :])))(
                                tw * 128, tw % 3), w, dma=True)
                        wos_state["next_load"] += 1
                    wos_state["next_abs"] += 1
                    t_woab[i] = sch.add("S", (lambda b, tc: (lambda e: e.activation(
                        b_wo[b][:, :], b_wo[b][:, :], ACTF.Abs,
                        accum_out=b_po[:, tc:tc + 1])))(i % 3, i),
                        [wos_state["loads"][i]])

                items0 = pair_quant_items(0)
                while items0:
                    dispense_q(items0)
                for P in range(NP):
                    items = pair_quant_items(P + 1) if P + 1 < NP else []

                    if P == 4:
                        t_wr1 = sch.add("V", lambda e: e.reduce_sum(
                            v_ti[:, 1:2], b_po[:, :], axis=AX.X), [t_woab[63]])
                        t_wr2 = sch.add("G", lambda e: e.reduce_sum(
                            v_ti[0:1, 5:6], v_ti[:, 1:2], axis=AX.C), [t_wr1])
                        t_wr3 = sch.add("Q", lambda e: e.dma_start(
                            d_sc1, v_ti[0:1, 5:6]), [t_wr2], dma=True)
                        t_wr4 = sch.add("Q", lambda e: e.dma_start(
                            v_ti[:, 7:8], bass.AP(d_sc1.tensor, 0, [[0, 128], [1, 1]])),
                            [t_wr3], dma=True)
                        t_sout = _seq(sch, "V", [
                            lambda e: e.tensor_scalar(v_sout[:, 0:1], v_ti[:, 7:8], 1.0 / N_WOUT,
                                                      EPS, ALU.mult, ALU.max),
                            lambda e: e.tensor_scalar(v_hsout[:, 0:1], v_sout[:, 0:1], 0.5,
                                                      None, ALU.mult),
                            lambda e: e.tensor_scalar(v_nhsout[:, 0:1], v_hsout[:, 0:1], -1.0,
                                                      None, ALU.mult),
                        ], [t_wr4])

                    for tt in range(NT):
                        g = P * NT + tt

                        def em_mm(e, P=P, tt=tt):
                            s = (P % 2) * 4
                            for dt in range(ND):
                                lhs = bass.AP(k1T, dt * T + tt * 128,
                                              [[ND * T, 128], [1, 128]])
                                e.matmul(ps_a[tt % 2][:, :], lhs,
                                         b_q8[s + 0][:, dt * 512:(dt + 1) * 512],
                                         start=(dt == 0), stop=(dt == ND - 1))
                                e.matmul(ps_c[tt % 2][:, :], lhs,
                                         b_q8[s + 1][:, dt * 512:(dt + 1) * 512],
                                         start=(dt == 0), stop=(dt == ND - 1))
                                e.matmul(ps_b[tt % 2][:, :], lhs,
                                         b_q8[s + 2][:, dt * 512:(dt + 1) * 512],
                                         start=(dt == 0), stop=(dt == ND - 1))
                                inst = e.matmul(ps_d[tt % 2][:, :], lhs,
                                                b_q8[s + 3][:, dt * 512:(dt + 1) * 512],
                                                start=(dt == 0), stop=(dt == ND - 1))
                            return inst
                        w = list(t_wq[P])
                        if g >= 2:
                            w.append(t_h2vw[g - 2][0])
                            w.append(t_h2vw[g - 2][1])
                        t_mmg[P][tt] = sch.add("T", em_mm, w)

                        for j in range(2):
                            w_idx = 2 * P + j
                            psv = (ps_a if j == 0 else ps_c)[tt % 2]
                            psg = (ps_b if j == 0 else ps_d)[tt % 2]
                            sl = b_sl[(2 * g + j) % 4]
                            h2b = b_h2[(2 * g + j) % 3]
                            t_sil = sch.add("S", (lambda sl=sl, psg=psg, tt=tt: (
                                lambda e: e.activation(
                                    sl[:, :], psg[:, :], ACTF.Silu,
                                    scale=v_c1[:, tt:tt + 1])))(), [t_mmg[P][tt]])
                            w2 = [t_sil]
                            if 2 * g + j >= 3:
                                go = 2 * g + j - 3
                                w2.append(t_h2sp[go // 2][go % 2])
                            t_h2vw[g][j] = _seq(sch, "V", [
                                (lambda h2b=h2b, psv=psv, sl=sl, tt=tt: lambda e: e.scalar_tensor_tensor(
                                    h2b[:, :], psv[:, :],
                                    v_c1[:, tt:tt + 1], sl[:, :],
                                    ALU.mult, ALU.mult))(),
                                (lambda h2b=h2b, j=j: lambda e: e.reduce_max(
                                    b_hm[j][:, 0:1], h2b[:, :], axis=AX.X,
                                    apply_absolute_value=True))(),
                                (lambda tt=tt, j=j: lambda e: e.tensor_max(
                                    v_hmax[:, tt:tt + 1], v_hmax[:, tt:tt + 1],
                                    b_hm[j][:, 0:1]))(),
                            ], w2)
                            t_h2sp[g][j] = sch.add("G", (lambda w_idx=w_idx, tt=tt, h2b=h2b: (
                                lambda e: e.dma_start(
                                    d_h2[tt * 128:(tt + 1) * 128,
                                         w_idx * 512:(w_idx + 1) * 512],
                                    h2b[:, :])))(), [t_h2vw[g][j]], dma=True)

                        dispense_q(items)
                        if P < 4:
                            dispense_wos()
                        else:
                            dispense_wo()

                _wire(nc, sch, sems)

        esC = ExitStack()
        with esC:
            b_wq8r = esC.enter_context(nc.sbuf_tensor("b_wq8r", [128, NH * 2048], F8))
            k2T = [esC.enter_context(nc.sbuf_tensor(f"k2T{i}", [128, NH * 128], BF16)) for i in range(2)]
            NH2R = 3
            c_h2i = [esC.enter_context(nc.sbuf_tensor(f"c_h2i{i}", [128, 512], F32)) for i in range(NH2R)]
            c_kt = [esC.enter_context(nc.sbuf_tensor(f"c_kt{i}", [128, 512], I32)) for i in range(2)]
            c_m1 = [esC.enter_context(nc.sbuf_tensor(f"c_m1{i}", [128, 512], BF16)) for i in range(2)]
            c_m2 = esC.enter_context(nc.sbuf_tensor("c_m2", [128, 512], F32))
            NKB = 16
            c_kb = [esC.enter_context(nc.sbuf_tensor(f"c_kb{i}", [128, 512], BF16)) for i in range(NKB)]
            c_x1 = esC.enter_context(nc.sbuf_tensor("c_x1", [128, D], F32))
            c_z2 = esC.enter_context(nc.sbuf_tensor("c_z2", [128, D], F32))
            c_zs = esC.enter_context(nc.sbuf_tensor("c_zs", [128, 8], F32))
            ps_trb2 = [esC.enter_context(nc.psum_tensor(f"ps_trbC{i}", [128, 1024], BF16)) for i in range(2)]
            ps_o = [esC.enter_context(nc.psum_tensor(f"ps_o{i}", [128, 512], F32)) for i in range(6)]

            sch = Sched(prev=nc._schB)

            t_c23 = _seq(sch, "V", [
                lambda e: e.tensor_scalar(v_c2[:, :], v_hmax[:, :], EPS, None, ALU.max),
                lambda e: e.tensor_scalar(v_c3[:, :], v_c2[:, :], 1.0 / QB, v_sout[:, 0:1],
                                          ALU.mult, ALU.mult),
                lambda e: e.reciprocal(v_Mh[:, :], v_c2[:, :]),
                lambda e: e.tensor_scalar(v_c2[:, :], v_Mh[:, :], QB, None, ALU.mult),
            ])
            s3 = d_wq8.rearrange("(a p) o -> p a o", p=128)
            o3 = b_wq8r[:, :].rearrange("p (a o) -> p a o", o=2048)
            t_wq8r_l = []
            for gidx in range(8):
                t_wq8r_l.append(sch.add(
                    ("Q", "R", "G")[gidx % 3],
                    (lambda g=gidx: (lambda e: e.dma_start(
                        o3[:, g * 8:(g + 1) * 8, :], s3[:, g * 8:(g + 1) * 8, :])))(),
                    dma=True))
            t_wq8r = t_wq8r_l

            t_x1cur = [None] * NT
            t_x1cur[0] = sch.add("G", lambda e: e.dma_start(c_x1[:, :], d_x1[0:128, :]), dma=True)

            t_k2e = {}
            t_qc = {}
            t_q4 = {}
            t_ktr2 = {}
            t_ev2 = [[None] * 4 for _ in range(NT)]
            t_mm2 = [[None] * 4 for _ in range(NT)]
            t_st = [None] * NT

            t_h2l = {}
            h2_state = {"n": 0}

            def pump_h2(upto):
                while h2_state["n"] < min(upto, NT * 16):
                    n = h2_state["n"]
                    h2_state["n"] = n + 1
                    w = [t_qc[n - NH2R]] if n >= NH2R else []
                    t_h2l[n] = sch.add("Q", (lambda n=n, b=n % NH2R: (lambda e: e.dma_start(
                        c_h2i[b][:, :], d_h2[(n // 16) * 128:(n // 16 + 1) * 128,
                                             (n % 16) * 512:(n % 16 + 1) * 512])))(),
                        w, dma=True)

            def build_quant(bt):
                for ch in range(16):
                    cg = bt * 16 + ch
                    pump_h2(cg + NH2R)
                    hv = c_h2i[cg % NH2R][:, :]

                    w = [t_h2l[cg], t_c23]
                    if cg >= 2:
                        w.append(t_q4[cg - 2])
                    t_q2 = sch.add("S", (lambda hv=hv, bt=bt, r=cg % 2: lambda e: e.activation(
                        c_kt[r][:, :], hv, ACTF.Copy, scale=v_c2[:, bt:bt + 1]))(), w)

                    w = [t_q2] + ([t_q4[cg - 2]] if cg >= 2 else [])
                    t_qc[cg] = _seq(sch, "V", [
                        (lambda hv=hv, bt=bt, r=cg % 2: lambda e: e.scalar_tensor_tensor(
                            c_m2[:, :], hv, v_c2[:, bt:bt + 1], c_kt[r][:, :],
                            ALU.mult, ALU.subtract))(),
                        (lambda r=cg % 2: lambda e: e.tensor_scalar(
                            c_m1[r][:, :], c_m2[:, :], -1.0, 0.5,
                            ALU.mult, ALU.is_gt))(),
                        (lambda r=cg % 2: lambda e: e.scalar_tensor_tensor(
                            c_m1[r][:, :], c_m2[:, :], 0.5,
                            c_m1[r][:, :], ALU.is_gt, ALU.subtract))(),
                    ], w)
                    w = [t_qc[cg]] + ([t_ktr2[cg - NKB]] if cg >= NKB else [])
                    t_q4[cg] = sch.add("V", (lambda kbb=cg % NKB, r=cg % 2: lambda e: e.tensor_add(
                        c_kb[kbb][:, :], c_kt[r][:, :], c_m1[r][:, :]))(), w)

            def build_transpose(bt, chs=range(16)):
                for ch in chs:
                    cg = bt * 16 + ch

                    def em_t2(e, kbb=cg % NKB, pb=cg % 2):
                        for j in range(4):
                            inst = e.transpose(ps_trb2[pb][:, j * 128:(j + 1) * 128],
                                               c_kb[kbb][:, j * 128:(j + 1) * 128],
                                               c_idb[:, :])
                        return inst
                    w = [t_q4[cg]]
                    if cg >= 2:
                        w.append(t_k2e[cg - 2])
                    t_ktr2[cg] = sch.add("T", em_t2, w)
                    t_k2e[cg] = sch.add("S", (lambda bt=bt, ch=ch, pb=cg % 2: (lambda e: e.copy(
                        k2T[bt % 2][:, ch * 512:(ch + 1) * 512], ps_trb2[pb][:, 0:512])))(),
                        [t_ktr2[cg]])

            build_quant(0)
            build_transpose(0)
            t_sq2o_prev = None

            for tt in range(NT):
                if tt + 1 < NT:
                    build_quant(tt + 1)

                def em_m2(e, tt=tt):
                    for ht in range(NH):
                        lhs = k2T[tt % 2][:, ht * 128:(ht + 1) * 128]
                        for oc in range(4):
                            inst = e.matmul(ps_o[(4 * tt + oc) % 6][:, :], lhs,
                                            b_wq8r[:, ht * 2048 + oc * 512:
                                                   ht * 2048 + (oc + 1) * 512],
                                            start=(ht == 0), stop=(ht == NH - 1))
                    return inst
                w = [t_k2e[tt * 16 + 15]]
                if tt == 0:
                    w.extend(t_wq8r)
                if tt >= 1:
                    w.append(t_ev2[tt - 1][0])
                    w.append(t_ev2[tt - 1][1])
                t_mmall = sch.add("T", em_m2, w)
                if tt + 1 < NT:
                    build_transpose(tt + 1)
                for oc in range(4):
                    t_mm2[tt][oc] = t_mmall

                    def em_z2(e, tt=tt, oc=oc):
                        return e.scalar_tensor_tensor(
                            c_z2[:, oc * 512:(oc + 1) * 512], ps_o[(4 * tt + oc) % 6][:, :],
                            v_c3[:, tt:tt + 1], c_x1[:, oc * 512:(oc + 1) * 512],
                            ALU.mult, ALU.add, accum_out=c_zs[:, oc:oc + 1])
                    w = [t_mmall, t_x1cur[tt]]
                    if tt >= 1 and oc == 0:
                        w.append(t_st[tt - 1])
                    t_ev2[tt][oc] = sch.add("V", em_z2, w)

                    t_sq2o = sch.add("S", (lambda oc=oc, tt=tt: (lambda e: e.activation(
                        k2T[tt % 2][:, 0:512], c_z2[:, oc * 512:(oc + 1) * 512], ACTF.Square,
                        accum_out=c_zs[:, 4 + oc:5 + oc])))(), [t_ev2[tt][oc]],
                        hard_waits=[t_sq2o_prev])
                    t_sq2o_prev = t_sq2o

                t_var2 = _seq(sch, "V", [
                    lambda e: e.reduce_sum(v_ti[:, 0:1], c_zs[:, 0:4], axis=AX.X),
                    lambda e: e.reduce_sum(v_ti[:, 1:2], c_zs[:, 4:8], axis=AX.X),
                    lambda e: e.tensor_scalar(v_ti[:, 2:3], v_ti[:, 0:1], INV_D, None, ALU.mult),
                    lambda e: e.tensor_scalar(v_ti[:, 3:4], v_ti[:, 1:2], INV_D, None, ALU.mult),
                    lambda e: e.tensor_mul(v_ti[:, 4:5], v_ti[:, 2:3], v_ti[:, 2:3]),
                    lambda e: e.tensor_sub(v_ti[:, 3:4], v_ti[:, 3:4], v_ti[:, 4:5]),
                ], [t_sq2o])
                t_sqr = sch.add("S", lambda e: e.activation(
                    v_ti[:, 5:6], v_ti[:, 3:4], ACTF.Sqrt, bias=c_eps[:, 0:1]), [t_var2])

                t_norm = _seq(sch, "V", [
                    lambda e: e.reciprocal(v_ti[:, 6:7], v_ti[:, 5:6]),
                    lambda e: e.tensor_scalar(c_z2[:, :], c_z2[:, :], v_ti[:, 2:3],
                                              v_ti[:, 6:7], ALU.subtract, ALU.mult),
                ], [t_sqr])
                t_st[tt] = sch.add("Q", (lambda tt=tt: (lambda e: e.dma_start(
                    d_out[tt * 128:(tt + 1) * 128, :], c_z2[:, :])))(),
                    [t_norm], dma=True)

                if tt + 1 < NT:
                    t_x1cur[tt + 1] = sch.add("G", (lambda tt=tt: (lambda e: e.dma_start(
                        c_x1[:, :], d_x1[(tt + 1) * 128:(tt + 2) * 128, :])))(),
                        [t_ev2[tt][3]], dma=True)

            _wire(nc, sch, sems)

    return nc


_NC_CACHE = {}


def kernel(L0_attn_x, L0_attn_Wq, L0_attn_bq, L0_attn_Wkv, L0_attn_bkv,
           L0_ffn_norm_w, L0_ffn_w_gv, L0_ffn_w_out):
    x = np.ascontiguousarray(np.asarray(L0_attn_x, dtype=np.float32))
    wkv = np.asarray(L0_attn_Wkv, dtype=np.float32)
    bkv = np.asarray(L0_attn_bkv, dtype=np.float32)
    wgv = np.asarray(L0_ffn_w_gv, dtype=np.float32)
    wout = np.asarray(L0_ffn_w_out, dtype=np.float32)
    nw = np.asarray(L0_ffn_norm_w, dtype=np.float32)

    wgvT = np.ascontiguousarray(wgv.T)
    woutT = np.ascontiguousarray(wout.T)
    wvT = np.ascontiguousarray(wkv[128:256, :].T)
    bv = np.ascontiguousarray(bkv[128:256].reshape(1, 128))
    nwb = np.ascontiguousarray(np.broadcast_to(nw, (128, D)))
    idf = np.eye(128, dtype=np.float32)
    idb = np.eye(128).astype(ml_dtypes.bfloat16)
    ones1 = np.ones((1, 128), dtype=np.float32)
    ones128 = np.ones((128, 1), dtype=np.float32)

    if "nc" not in _NC_CACHE:
        _NC_CACHE["nc"] = build_nc()
    nc = _NC_CACHE["nc"]

    in_maps = []
    for c in range(8):
        in_maps.append({
            "x": np.ascontiguousarray(x[c]),
            "xT": np.ascontiguousarray(
                x[c].reshape(16, 128, 16, 128).transpose(0, 3, 2, 1).reshape(T, D)),
            "wgvT": wgvT, "woutT": woutT, "wvT": wvT, "bv": bv,
            "nwb": nwb, "idf": idf, "idb": idb,
            "ones1": ones1, "ones128": ones128,
            "epsv": np.full((128, 1), EPS, dtype=np.float32),
        })
    res = run_bass_kernel_spmd(nc, in_maps, core_ids=list(range(8)))
    out = np.stack([np.asarray(res.results[c]["out"]) for c in range(8)], axis=0)
    return out.astype(np.float32)


# version 11 (final)
# speedup vs baseline: 1.0204x; 1.0204x over previous
import numpy as np
import ml_dtypes
from contextlib import ExitStack

import concourse.bass as bass
import concourse.mybir as mybir
import concourse.bass_utils as _bass_utils
from concourse.bass_utils import run_bass_kernel_spmd

if not getattr(_bass_utils, "_ldwopt_patched", False):
    _orig_run_command = _bass_utils.run_command

    def _run_command_ldwopt(cmd, **kw):
        cmd = ["--enable-ldw-opt=true" if c == "--enable-ldw-opt=false" else c
               for c in cmd]
        return _orig_run_command(cmd, **kw)

    _bass_utils.run_command = _run_command_ldwopt
    _bass_utils._ldwopt_patched = True

F32 = mybir.dt.float32
BF16 = mybir.dt.bfloat16
F8 = mybir.dt.float8e4
I32 = mybir.dt.int32
ALU = mybir.AluOpType
ACTF = mybir.ActivationFunctionType
AX = mybir.AxisListType

T = 2048
D = 2048
H = 8192
O1 = 2 * H
O2 = D
NT = T // 128
ND = D // 128
NH = H // 128
NWP = H // 512
EPS = 1e-5
QB = 127.0
INV_D = 1.0 / D
N_WGV = float(O1 * D)
N_WOUT = float(O2 * H)

ENGS = ("T", "V", "S", "Q", "G")


NQSEM = 8
QUEUES = ("Q", "G", "R")


class _Op:
    __slots__ = ("eng", "emit", "waits", "dma", "ndma", "need_inc", "sem", "val",
                 "prev_val", "hard")

    def __init__(self, eng, emit, waits, dma, ndma):
        self.eng, self.emit, self.waits, self.dma = eng, emit, list(waits), dma
        self.ndma = ndma
        self.need_inc = dma
        self.sem = None
        self.val = None
        self.prev_val = 0


class Sched:

    def __init__(self, prev=None):
        self.ops = []
        self.prev = prev

    def add(self, eng, emit, waits=(), dma=False, ndma=1, hard_waits=()):
        op = _Op(eng, emit, [w for w in waits if w is not None], dma, ndma)
        op.hard = [w for w in hard_waits if w is not None]
        self.ops.append(op)
        return op

    def finalize(self):
        prev = {"V": None, "S": None, "G": None}
        for op in self.ops:
            if op.eng in prev:
                if prev[op.eng] is not None:
                    op.hard.append(prev[op.eng])
                prev[op.eng] = op
        lastc = {}
        for op in self.ops:
            if not op.dma and op.eng in ("T", "V", "S", "G"):
                lastc[op.eng] = op
        for op in lastc.values():
            op.need_inc = True
        for op in self.ops:
            for w in op.waits:
                if w.eng != op.eng or w.dma:
                    w.need_inc = True
            for w in op.hard:
                w.need_inc = True
        if self.prev is not None:
            cnt = dict(self.prev.final_cnt)
            qrot = dict(self.prev.final_qrot)
            qcnt = dict(self.prev.qtot)
            self.entry_waits = dict(cnt)
            for k, v in qcnt.items():
                self.entry_waits[k] = v
        else:
            cnt = {e: 0 for e in ("T", "V", "S", "G")}
            qrot = {q: 0 for q in QUEUES}
            qcnt = {(q, i): 0 for q in QUEUES for i in range(NQSEM)}
            self.entry_waits = {}
        self.qtot = qcnt
        for op in self.ops:
            if op.dma:
                i = qrot[op.eng]
                qrot[op.eng] = (i + 1) % NQSEM
                op.sem = (op.eng, i)
                op.prev_val = qcnt[op.sem]
                qcnt[op.sem] += 16 * op.ndma
                op.val = qcnt[op.sem]
            elif op.need_inc:
                cnt[op.eng] += 1
                op.sem = op.eng
                op.val = cnt[op.eng]
        for op in self.ops:
            ws = {}
            for w in op.waits:
                if not w.need_inc:
                    continue
                if (not w.dma) and w.eng == op.eng:
                    continue
                ws[w.sem] = max(ws.get(w.sem, 0), w.val)
            for w in op.hard:
                ws[w.sem] = max(ws.get(w.sem, 0), w.val)
            op.waits = ws
        self.final_cnt = cnt
        self.final_qrot = qrot

    def replay(self, eng_name, eng, sems):
        names = ("S", "R") if eng_name == "S" else (eng_name,)
        last = {}
        for sk, v in self.entry_waits.items():
            if v > 0 and sk not in names:
                eng.wait_ge(sems[sk], v)
                last[sk] = v
            else:
                last[sk] = v
        for op in self.ops:
            if op.eng not in names:
                continue
            if op.dma and op.prev_val > 0 and last.get(op.sem, 0) < op.prev_val:
                eng.wait_ge(sems[op.sem], op.prev_val)
                last[op.sem] = op.prev_val
            for sk, v in op.waits.items():
                if v > last.get(sk, 0):
                    eng.wait_ge(sems[sk], v)
                    last[sk] = v
            out = op.emit(eng)
            if op.dma:
                insts = out if isinstance(out, list) else [out]
                assert len(insts) == op.ndma, (op.eng, len(insts), op.ndma)
                for inst in insts:
                    inst.then_inc(sems[op.sem], 16)
            elif op.need_inc:
                out.then_inc(sems[op.sem], 1)
        for qn in names:
            if qn in QUEUES:
                for i in range(NQSEM):
                    tot = self.qtot[(qn, i)]
                    if tot > last.get((qn, i), 0):
                        eng.wait_ge(sems[(qn, i)], tot)


def _seq(sch, eng, emits, waits=()):
    t = None
    for i, em in enumerate(emits):
        t = sch.add(eng, em, waits if i == 0 else ())
    return t


def _wire(nc, sched, sems):
    sched.finalize()
    with nc.Block() as blk:
        blk.tensor(lambda e: sched.replay("T", e, sems))
        blk.vector(lambda e: sched.replay("V", e, sems))
        blk.scalar(lambda e: sched.replay("S", e, sems))
        blk.sync(lambda e: sched.replay("Q", e, sems))
        blk.gpsimd(lambda e: sched.replay("G", e, sems))


def build_nc():
    nc = bass.Bass("TRN2", target_bir_lowering=False)

    d_x = nc.dram_tensor("x", [T, D], F32, kind="ExternalInput").ap()
    d_xT = nc.dram_tensor("xT", [D, T], F32, kind="ExternalInput").ap()
    d_wgvT = nc.dram_tensor("wgvT", [D, O1], F32, kind="ExternalInput").ap()
    d_woutT = nc.dram_tensor("woutT", [H, O2], F32, kind="ExternalInput").ap()
    d_wvT = nc.dram_tensor("wvT", [D, 128], F32, kind="ExternalInput").ap()
    d_bv = nc.dram_tensor("bv", [1, 128], F32, kind="ExternalInput").ap()
    d_nwb = nc.dram_tensor("nwb", [128, D], F32, kind="ExternalInput").ap()
    d_idf = nc.dram_tensor("idf", [128, 128], F32, kind="ExternalInput").ap()
    d_idb = nc.dram_tensor("idb", [128, 128], BF16, kind="ExternalInput").ap()
    d_ones1 = nc.dram_tensor("ones1", [1, 128], F32, kind="ExternalInput").ap()
    d_ones128 = nc.dram_tensor("ones128", [128, 1], F32, kind="ExternalInput").ap()
    d_epsv = nc.dram_tensor("epsv", [128, 1], F32, kind="ExternalInput").ap()
    d_out = nc.dram_tensor("out", [T, D], F32, kind="ExternalOutput").ap()
    d_h2 = nc.dram_tensor("h2sp", [T, H], F32, kind="Internal").ap()
    d_x1 = nc.dram_tensor("x1sp", [T, D], F32, kind="Internal").ap()
    d_wq8 = nc.dram_tensor("wq8sp", [H, O2], F8, kind="Internal").ap()
    d_sc1 = nc.dram_tensor("sc1sp", [1, 1], F32, kind="Internal").ap()

    es = ExitStack()
    with es:
        c_idb = es.enter_context(nc.sbuf_tensor("c_idb", [128, 128], BF16))
        c_ones1 = es.enter_context(nc.sbuf_tensor("c_ones1", [1, 128], F32))
        c_ones128 = es.enter_context(nc.sbuf_tensor("c_ones128", [128, 1], F32))
        c_bv = es.enter_context(nc.sbuf_tensor("c_bv", [1, 128], F32))
        c_eps = es.enter_context(nc.sbuf_tensor("c_eps", [128, 1], F32))
        v_Mh = es.enter_context(nc.sbuf_tensor("v_Mh", [128, NT], F32))
        v_c1 = es.enter_context(nc.sbuf_tensor("v_c1", [128, NT], F32))
        v_hmax = es.enter_context(nc.sbuf_tensor("v_hmax", [128, NT], F32))
        v_c2 = es.enter_context(nc.sbuf_tensor("v_c2", [128, NT], F32))
        v_c3 = es.enter_context(nc.sbuf_tensor("v_c3", [128, NT], F32))
        v_sgv = es.enter_context(nc.sbuf_tensor("v_sgv", [128, 1], F32))
        v_hsgv = es.enter_context(nc.sbuf_tensor("v_hsgv", [128, 1], F32))
        v_sout = es.enter_context(nc.sbuf_tensor("v_sout", [128, 1], F32))
        v_hsout = es.enter_context(nc.sbuf_tensor("v_hsout", [128, 1], F32))
        v_nhsout = es.enter_context(nc.sbuf_tensor("v_nhsout", [128, 1], F32))
        v_ti = es.enter_context(nc.sbuf_tensor("v_ti", [128, 8], F32))


        sems = {e: es.enter_context(nc.semaphore(f"sem_{e}")) for e in ("T", "V", "S", "G")}
        for qe in QUEUES:
            for i in range(NQSEM):
                sems[(qe, i)] = es.enter_context(nc.semaphore(f"sem_{qe}{i}"))

        esAB = ExitStack()
        with esAB:
            k1T = esAB.enter_context(nc.sbuf_tensor("k1T", [128, ND * T], BF16))
            b_wo8 = [esAB.enter_context(nc.sbuf_tensor(f"b_wo8{i}", [128, 2048], F8)) for i in range(1)]


            esA = ExitStack()
            with esA:
                c_wvT = esA.enter_context(nc.sbuf_tensor("c_wvT", [128, ND * 128], F32))
                c_nwb = esA.enter_context(nc.sbuf_tensor("c_nwb", [128, D], F32))
                a_x = [esA.enter_context(nc.sbuf_tensor(f"a_x{i}", [128, D], F32)) for i in range(3)]
                a_xT = esA.enter_context(nc.sbuf_tensor("a_xT", [128, ND * 128], F32))
                a_z = esA.enter_context(nc.sbuf_tensor("a_z", [128, D], F32))
                a_u = esA.enter_context(nc.sbuf_tensor("a_u", [128, D], F32))
                a_kt = esA.enter_context(nc.sbuf_tensor("a_kt", [128, D], I32))
                a_m1 = esA.enter_context(nc.sbuf_tensor("a_m1", [128, D], F32))
                a_m2 = esA.enter_context(nc.sbuf_tensor("a_m2", [128, D], F32))
                a_kb = esA.enter_context(nc.sbuf_tensor("a_kb", [128, D], BF16))
                a_gvB = esA.enter_context(nc.sbuf_tensor("a_gvB", [128, 4 * D], F32))
                a_y2 = esA.enter_context(nc.sbuf_tensor("a_y2", [128, D], F32))
                a_pgv = esA.enter_context(nc.sbuf_tensor("a_pgv", [128, 64], F32))
                st = esA.enter_context(nc.sbuf_tensor("a_st", [128, 16], F32))
                ps_v = esA.enter_context(nc.psum_tensor("ps_v", [128, 512], F32))
                ps_s = esA.enter_context(nc.psum_tensor("ps_s", [128, 512], F32))
                ps_tr = [esA.enter_context(nc.psum_tensor(f"ps_tr{i}", [128, 512], F32)) for i in range(2)]
                ps_trb = [esA.enter_context(nc.psum_tensor(f"ps_trbA{i}", [128, 1024], BF16)) for i in range(2)]

                sch = Sched()
                nc._schA = sch

                t_pre_all = []
                for dst, src in [
                    (c_idb[:, :], d_idb),
                    (c_ones1[:, :], d_ones1), (c_ones128[:, :], d_ones128),
                    (c_bv[:, :], d_bv), (c_nwb[:, :], d_nwb),
                    (c_eps[:, :], d_epsv),
                    (c_wvT[:, :].rearrange("p (a j) -> p a j", j=128),
                     d_wvT.rearrange("(a p) j -> p a j", p=128)),
                ]:
                    t_pre_all.append(sch.add("Q", (lambda dd, ss: (lambda e: e.dma_start(dd, ss)))(dst, src), dma=True))
                t_pre = t_pre_all[-1]

                NSG = 128
                t_gvl = [[None, None] for _ in range(NSG // 2)]
                t_gvp = [None] * (NSG // 2)
                t_x = [None] * NT
                t_z = [None] * NT
                t_u = [None] * NT
                t_ysp = [None] * NT
                t_qk = [None] * NT
                t_kb = [None] * NT
                t_k1e = [[None, None] for _ in range(NT)]
                t_xTl = [None] * NT
                t_vmmA = [None] * NT
                t_ktr = [[None, None] for _ in range(NT)]
                t_cact = [None] * NT

                def emit_sload(p):
                    t = 2 * p
                    rb, cs = t // 8, t % 8
                    s0 = t % 4
                    w = [t_gvp[p - 2]] if p >= 2 else list(t_pre_all)
                    qn = ("Q", "R", "G")[p % 3]
                    srcp = d_wgvT[rb * 128:(rb + 1) * 128, cs * 2048:(cs + 2) * 2048]
                    t_gvl[p][0] = sch.add(qn, (lambda s0=s0, ss=srcp: (lambda e: e.dma_start(
                        a_gvB[:, s0 * 2048:(s0 + 2) * 2048], ss)))(),
                        w, dma=True)

                def emit_sabs(p):
                    s0 = (2 * p) % 4
                    t_gvp[p] = sch.add("S", (lambda s0=s0, p=p: (lambda e: e.activation(
                        a_gvB[:, s0 * 2048:s0 * 2048 + 4096],
                        a_gvB[:, s0 * 2048:s0 * 2048 + 4096], ACTF.Abs,
                        accum_out=a_pgv[:, p:p + 1])))(), [t_gvl[p][0]])

                def emit_xload(tt):
                    w = [t_zsq[tt - 3]] if tt >= 3 else list(t_pre_all)
                    t_x[tt] = sch.add("Q", (lambda b, r0: (lambda e: e.dma_start(
                        a_x[b][:, :], d_x[r0:r0 + 128, :])))(tt % 3, tt * 128), w, dma=True)

                def emit_xT(tt):
                    w = [t_vmmA[tt - 1]] if tt >= 1 else list(t_pre_all)
                    t_xTl[tt] = sch.add("Q", (lambda r0=tt * 128: (lambda e: e.dma_start(
                        a_xT[:, :], d_xT[r0:r0 + 128, :])))(),
                        w, dma=True)

                def emit_vmm(tt):
                    def em_v(e):
                        for dt in range(ND):
                            e.matmul(ps_v[:, 0:128],
                                     a_xT[:, dt * 128:(dt + 1) * 128],
                                     c_wvT[:, dt * 128:(dt + 1) * 128],
                                     start=(dt == 0), stop=False)
                        return e.matmul(ps_v[:, 0:128], c_ones1[0:1, :], c_bv[0:1, :],
                                        start=False, stop=True)
                    w = [t_xTl[tt]] + ([t_z[tt - 1]] if tt >= 1 else [])
                    t_vmmA[tt] = sch.add("T", em_v, w)
                    return t_vmmA[tt]

                def emit_z(tt, t_vmm):
                    def em_z(e, b=tt % 3):
                        vbc = ps_v[:, 0:128].unsqueeze(1).broadcast_to((128, ND, 128))
                        return e.scalar_tensor_tensor(a_z[:, :], a_x[b][:, :], 1.0, vbc,
                                                      ALU.mult, ALU.add,
                                                      accum_out=st[:, 0:1])
                    w = [t_vmm, t_x[tt]] + ([t_zsq[tt - 1]] if tt >= 1 else [])
                    t_z[tt] = sch.add("V", em_z, w)
                    t_zsq[tt] = sch.add("S", (lambda tt=tt: lambda e: e.activation(
                        a_x[tt % 3][:, :], a_z[:, :], ACTF.Square,
                        accum_out=st[:, 1:2]))(), [t_z[tt]])

                def emit_var(tt):
                    t_var = _seq(sch, "V", [
                        lambda e: e.tensor_scalar(st[:, 4:5], st[:, 0:1], INV_D, None, ALU.mult),
                        lambda e: e.tensor_scalar(st[:, 2:3], st[:, 1:2], INV_D, None, ALU.mult),
                        lambda e: e.tensor_mul(st[:, 12:13], st[:, 4:5], st[:, 4:5]),
                        lambda e: e.tensor_sub(st[:, 2:3], st[:, 2:3], st[:, 12:13]),
                    ], [t_zsq[tt]])
                    t_sq1 = sch.add("S", lambda e: e.activation(
                        st[:, 3:4], st[:, 2:3], ACTF.Sqrt, bias=c_eps[:, 0:1]), [t_var])
                    t_y[tt] = sch.add("V", lambda e: e.reciprocal(
                        st[:, 8:9], st[:, 3:4]), [t_sq1])

                def emit_iter(tt):
                    nxt = tt + 1 if tt + 1 < NT else None
                    if nxt is not None:
                        emit_xT(nxt)
                        emit_xload(nxt)
                        emit_sload(nxt * 4 + 0)
                        emit_sload(nxt * 4 + 1)

                    w = [t_y[tt]] + ([t_qk[tt - 1]] if tt >= 1 else [])
                    t_p = _seq(sch, "V", [
                        lambda e: e.scalar_tensor_tensor(a_u[:, :], a_z[:, :],
                                                         st[:, 4:5], c_nwb[:, :],
                                                         ALU.subtract, ALU.mult),
                        lambda e: e.reduce_max(st[:, 10:11], a_u[:, :], axis=AX.X,
                                               apply_absolute_value=True),
                    ], w)
                    w = [t_ysp[tt - 1]] if tt >= 1 else []
                    t_y2 = _seq(sch, "V", [
                        lambda e: e.tensor_scalar(st[:, 5:6], st[:, 8:9], 2.0, None,
                                                  ALU.mult),
                        lambda e: e.tensor_scalar(a_y2[:, :], a_z[:, :], st[:, 4:5],
                                                  st[:, 5:6], ALU.subtract, ALU.mult),
                    ], w)
                    t_ysp[tt] = sch.add("G", (lambda r0: (lambda e: e.dma_start(
                        d_x1[r0:r0 + 128, :], a_y2[:, :])))(tt * 128),
                        [t_y2], dma=True)
                    t_c1 = _seq(sch, "V", [
                        lambda e: e.tensor_mul(st[:, 6:7], st[:, 8:9], st[:, 8:9]),
                        lambda e: e.tensor_mul(st[:, 6:7], st[:, 6:7], st[:, 2:3]),
                    ], [t_p])
                    t_sq2 = sch.add("S", lambda e: e.activation(
                        st[:, 7:8], st[:, 6:7], ACTF.Sqrt, bias=c_eps[:, 0:1]),
                        [t_c1])
                    t_cact[tt] = _seq(sch, "V", [
                        lambda e: e.reciprocal(st[:, 11:12], st[:, 7:8]),
                        lambda e: e.tensor_mul(st[:, 12:13], st[:, 8:9], st[:, 11:12]),
                        lambda e: e.tensor_mul(st[:, 14:15], st[:, 12:13], st[:, 10:11]),
                        (lambda tt=tt: lambda e: e.tensor_scalar(
                            v_Mh[:, tt:tt + 1], st[:, 14:15], EPS, None, ALU.max))(),
                        (lambda tt=tt: lambda e: e.reciprocal(
                            st[:, 13:14], v_Mh[:, tt:tt + 1]))(),
                        lambda e: e.tensor_scalar(st[:, 9:10], st[:, 13:14], QB,
                                                  st[:, 12:13], ALU.mult, ALU.mult),
                    ], [t_sq2])
                    w = [t_cact[tt]] + ([t_kb[tt - 1]] if tt >= 1 else [])
                    t_qk[tt] = sch.add("S", lambda e: e.activation(
                        a_kt[:, :], a_u[:, :], ACTF.Copy, scale=st[:, 9:10]), w)
                    if nxt is not None:
                        emit_sabs(nxt * 4 + 0)
                        emit_sabs(nxt * 4 + 1)
                        emit_sload(nxt * 4 + 2)
                        emit_sload(nxt * 4 + 3)
                        t_vmm = emit_vmm(nxt)
                        emit_z(nxt, t_vmm)
                    w = [t_qk[tt]] + ([t_ktr[tt - 1][1]] if tt >= 1 else [])
                    t_kb[tt] = _seq(sch, "V", [
                        lambda e: e.scalar_tensor_tensor(a_m2[:, :], a_u[:, :],
                                                         st[:, 9:10], a_kt[:, :],
                                                         ALU.mult, ALU.subtract),
                        lambda e: e.tensor_scalar(a_m1[:, :], a_m2[:, :], -1.0, 0.5,
                                                  ALU.mult, ALU.is_gt),
                        lambda e: e.scalar_tensor_tensor(a_m1[:, :], a_m2[:, :], 0.5,
                                                         a_m1[:, :], ALU.is_gt,
                                                         ALU.subtract),
                        lambda e: e.tensor_add(a_kb[:, :], a_kt[:, :], a_m1[:, :]),
                    ], w)
                    if nxt is not None:
                        emit_var(nxt)
                        emit_sabs(nxt * 4 + 2)
                        emit_sabs(nxt * 4 + 3)
                    for g in range(2):
                        def em_kt2(e, g=g):
                            for j in range(8):
                                dt = g * 8 + j
                                inst = e.transpose(ps_trb[g][:, j * 128:(j + 1) * 128],
                                                   a_kb[:, dt * 128:(dt + 1) * 128],
                                                   c_idb[:, :])
                            return inst
                        w = [t_kb[tt]]
                        if tt >= 1:
                            w.append(t_k1e[tt - 1][g])
                        t_ktr[tt][g] = sch.add("T", em_kt2, w)

                        def em_ke(e, g=g, tt=tt):
                            out_ap = bass.AP(k1T, (g * 8) * T + tt * 128,
                                             [[ND * T, 128], [T, 8], [1, 128]])
                            return e.tensor_copy(out_ap, ps_trb[g][:, :])
                        t_k1e[tt][g] = sch.add("V", em_ke, [t_ktr[tt][g]])

                t_zsq = [None] * NT
                t_y = [None] * NT
                emit_xT(0)
                emit_xload(0)
                emit_z(0, emit_vmm(0))
                emit_var(0)
                emit_sload(0)
                emit_sload(1)
                emit_sabs(0)
                emit_sabs(1)
                emit_sload(2)
                emit_sload(3)
                emit_sabs(2)
                emit_sabs(3)
                for tt in range(NT):
                    emit_iter(tt)

                t_r1 = sch.add("V", lambda e: e.reduce_sum(
                    v_ti[:, 0:1], a_pgv[:, :], axis=AX.X), [t_gvp[NSG // 2 - 1]])
                t_r2 = sch.add("T", lambda e: e.matmul(
                    ps_s[0:1, 0:1], v_ti[:, 0:1], c_ones128[:, 0:1],
                    start=True, stop=True), [t_r1, t_z[NT - 1]])
                t_r3 = sch.add("V", lambda e: e.tensor_copy(v_ti[0:1, 4:5], ps_s[0:1, 0:1]),
                               [t_r2])
                t_r4 = sch.add("T", lambda e: e.matmul(
                    ps_v[:, 0:1], c_ones1[0:1, :], v_ti[0:1, 4:5],
                    start=True, stop=True), [t_r3])

                _seq(sch, "V", [
                    lambda e: e.tensor_scalar(v_sgv[:, 0:1], ps_v[:, 0:1], 1.0 / N_WGV, EPS,
                                              ALU.mult, ALU.max),
                    lambda e: e.tensor_scalar(v_hsgv[:, 0:1], v_sgv[:, 0:1], 0.5, None, ALU.mult),
                    lambda e: e.tensor_scalar(v_c1[:, :], v_Mh[:, :], 1.0 / QB, v_sgv[:, 0:1],
                                              ALU.mult, ALU.mult),
                    lambda e: e.memset(v_hmax[:, :], 0.0),
                ], [t_r4, t_cact[NT - 1]])

                _wire(nc, sch, sems)

            esB = ExitStack()
            with esB:
                b_wq = [esB.enter_context(nc.sbuf_tensor(f"b_wq{i}", [128, 8 * 512], F32)) for i in range(2)]
                b_q8 = [esB.enter_context(nc.sbuf_tensor(f"b_q8{i}", [128, ND * 512], F8)) for i in range(8)]
                b_h2 = [esB.enter_context(nc.sbuf_tensor(f"b_h2{i}", [128, 512], F32)) for i in range(3)]
                b_sl = [esB.enter_context(nc.sbuf_tensor(f"b_sl{i}", [128, 512], F32)) for i in range(4)]
                b_wo = [esB.enter_context(nc.sbuf_tensor(f"b_wo{i}", [128, 2048], F32)) for i in range(3)]
                b_po = esB.enter_context(nc.sbuf_tensor("b_po", [128, 64], F32))
                b_wm = esB.enter_context(nc.sbuf_tensor("b_wm", [128, 2048], BF16))
                b_hm = [esB.enter_context(nc.sbuf_tensor(f"b_hm{i}", [128, 1], F32)) for i in range(2)]
                ps_a = [esB.enter_context(nc.psum_tensor(f"ps_a{i}", [128, 512], F32)) for i in range(2)]
                ps_b = [esB.enter_context(nc.psum_tensor(f"ps_b{i}", [128, 512], F32)) for i in range(2)]
                ps_c = [esB.enter_context(nc.psum_tensor(f"ps_c{i}", [128, 512], F32)) for i in range(2)]
                ps_d = [esB.enter_context(nc.psum_tensor(f"ps_d{i}", [128, 512], F32)) for i in range(2)]

                sch = Sched(prev=nc._schA)
                nc._schB = sch

                NP = NWP // 2
                t_wq = [[None] * 4 for _ in range(NP)]
                t_mmg = [[None] * NT for _ in range(NP)]
                t_h2vw = [[None, None] for _ in range(NP * NT)]
                t_h2sp = [[None, None] for _ in range(NP * NT)]
                t_woab = [None] * 64
                t_woq = [None] * 64
                t_woqo = [None] * 64
                t_sout = None

                wgvT_3d = d_wgvT.rearrange("(a p) o -> p a o", p=128)
                qctr = [0]
                t_qld = {}
                t_q8q = {}

                def emit_qload(qg):
                    P, rr = divmod(qg, 8)
                    h, qp = divmod(rr, 2)
                    w_idx = 2 * P + (h % 2)
                    col0 = (h // 2) * H + w_idx * 512
                    w = [t_q8q[qg - 2]] if qg >= 2 else []
                    t_qld[qg] = sch.add("Q", (lambda b=qg % 2, col0=col0, qp=qp: (
                        lambda e: e.dma_start(
                            b_wq[b][:, :].rearrange("p (a j) -> p a j", j=512),
                            wgvT_3d[:, qp * 8:(qp + 1) * 8, col0:col0 + 512])))(),
                        w, dma=True)

                def pair_quant_items(P):
                    items = []
                    for h in range(4):
                        qslot = (P % 2) * 4 + (h % 2) + 2 * (h // 2)
                        for qp in range(2):
                            qg = qctr[0]
                            qctr[0] += 1
                            for c2 in range(2):
                                items.append((qg, h, qslot, qp, c2, P))
                    return items

                def dispense_q(items):
                    for _ in range(2):
                        if not items:
                            return
                        qg, h, qslot, qp, c2, P = items.pop(0)
                        if c2 == 0:
                            if qg not in t_qld:
                                emit_qload(qg)
                            if qg + 1 < 8 * NP and (qg + 1) not in t_qld:
                                emit_qload(qg + 1)
                        qbuf = b_q8[qslot]
                        w = []
                        if c2 == 0:
                            w = [t_qld[qg]]
                            if P >= 2 and qp == 0:
                                w.append(t_mmg[P - 2][NT - 1])
                        sch.add("V", (lambda b=qg % 2, c2=c2: (lambda e: e.tensor_scalar(
                            b_wm[:, :], b_wq[b][:, c2 * 2048:(c2 + 1) * 2048],
                            -1.0, v_hsgv[:, 0:1], ALU.mult, ALU.is_gt)))(), w)
                        tok = sch.add("V", (lambda b=qg % 2, qbuf=qbuf, qp=qp, c2=c2: (
                            lambda e: e.scalar_tensor_tensor(
                                qbuf[:, qp * 4096 + c2 * 2048:qp * 4096 + (c2 + 1) * 2048],
                                b_wq[b][:, c2 * 2048:(c2 + 1) * 2048],
                                v_hsgv[:, 0:1], b_wm[:, :], ALU.is_gt, ALU.subtract)))(), [])
                        if c2 == 1:
                            t_q8q[qg] = tok
                            if qp == 1:
                                t_wq[P][h] = tok

                wo_queue = []
                wo_state = {"next_load": 0, "next_q": 0, "loads": {}}

                def _emit_wo_load(tw):
                    w = ([t_woq[tw - 3]] if tw >= 3 else []) + [t_sout]
                    wo_state["loads"][tw] = sch.add(
                        "R", (lambda r0, b: (lambda e: e.dma_start(
                            b_wo[b][:, :], d_woutT[r0:r0 + 128, :])))(tw * 128, tw % 3),
                        w, dma=True)

                def dispense_wo():
                    i = wo_state["next_q"]
                    if i >= 64 or t_sout is None:
                        return
                    while wo_state["next_load"] < min(i + 3, 64):
                        _emit_wo_load(wo_state["next_load"])
                        wo_state["next_load"] += 1
                    tw = i
                    wo_state["next_q"] += 1
                    w2 = [wo_state["loads"][tw]] + ([t_woqo[tw - 1]] if tw >= 1 else [])
                    sch.add("V", (lambda b=tw % 3: lambda e: e.tensor_scalar(
                        b_wm[:, :], b_wo[b][:, :], -1.0,
                        v_hsout[:, 0:1], ALU.mult, ALU.is_gt))(), w2)
                    t_woq[tw] = sch.add("V", (lambda b=tw % 3, b8=0: lambda e: e.scalar_tensor_tensor(
                        b_wo8[b8][:, :], b_wo[b][:, :],
                        v_hsout[:, 0:1], b_wm[:, :],
                        ALU.is_gt, ALU.subtract))(), [])
                    t_woqo[tw] = sch.add("G", (lambda r0, b: (lambda e: e.dma_start(
                        d_wq8[r0:r0 + 128, :], b_wo8[b][:, :])))(tw * 128, 0),
                        [t_woq[tw]], dma=True)

                wos_state = {"next_load": 0, "next_abs": 0, "loads": {}}

                def dispense_wos():
                    i = wos_state["next_abs"]
                    if i >= 64:
                        return
                    while wos_state["next_load"] < min(i + 3, 64):
                        tw = wos_state["next_load"]
                        w = [t_woab[tw - 3]] if tw >= 3 else []
                        wos_state["loads"][tw] = sch.add(
                            "R", (lambda r0, b: (lambda e: e.dma_start(
                                b_wo[b][:, :], d_woutT[r0:r0 + 128, :])))(
                                tw * 128, tw % 3), w, dma=True)
                        wos_state["next_load"] += 1
                    wos_state["next_abs"] += 1
                    t_woab[i] = sch.add("S", (lambda b, tc: (lambda e: e.activation(
                        b_wo[b][:, :], b_wo[b][:, :], ACTF.Abs,
                        accum_out=b_po[:, tc:tc + 1])))(i % 3, i),
                        [wos_state["loads"][i]])

                items0 = pair_quant_items(0)
                while items0:
                    dispense_q(items0)
                for P in range(NP):
                    items = pair_quant_items(P + 1) if P + 1 < NP else []

                    if P == 4:
                        t_wr1 = sch.add("V", lambda e: e.reduce_sum(
                            v_ti[:, 1:2], b_po[:, :], axis=AX.X), [t_woab[63]])
                        t_wr2 = sch.add("G", lambda e: e.reduce_sum(
                            v_ti[0:1, 5:6], v_ti[:, 1:2], axis=AX.C), [t_wr1])
                        t_wr3 = sch.add("Q", lambda e: e.dma_start(
                            d_sc1, v_ti[0:1, 5:6]), [t_wr2], dma=True)
                        t_wr4 = sch.add("Q", lambda e: e.dma_start(
                            v_ti[:, 7:8], bass.AP(d_sc1.tensor, 0, [[0, 128], [1, 1]])),
                            [t_wr3], dma=True)
                        t_sout = _seq(sch, "V", [
                            lambda e: e.tensor_scalar(v_sout[:, 0:1], v_ti[:, 7:8], 1.0 / N_WOUT,
                                                      EPS, ALU.mult, ALU.max),
                            lambda e: e.tensor_scalar(v_hsout[:, 0:1], v_sout[:, 0:1], 0.5,
                                                      None, ALU.mult),
                            lambda e: e.tensor_scalar(v_nhsout[:, 0:1], v_hsout[:, 0:1], -1.0,
                                                      None, ALU.mult),
                        ], [t_wr4])

                    for tt in range(NT):
                        g = P * NT + tt

                        def em_mm(e, P=P, tt=tt):
                            s = (P % 2) * 4
                            for dt in range(ND):
                                lhs = bass.AP(k1T, dt * T + tt * 128,
                                              [[ND * T, 128], [1, 128]])
                                e.matmul(ps_a[tt % 2][:, :], lhs,
                                         b_q8[s + 0][:, dt * 512:(dt + 1) * 512],
                                         start=(dt == 0), stop=(dt == ND - 1))
                                e.matmul(ps_c[tt % 2][:, :], lhs,
                                         b_q8[s + 1][:, dt * 512:(dt + 1) * 512],
                                         start=(dt == 0), stop=(dt == ND - 1))
                                e.matmul(ps_b[tt % 2][:, :], lhs,
                                         b_q8[s + 2][:, dt * 512:(dt + 1) * 512],
                                         start=(dt == 0), stop=(dt == ND - 1))
                                inst = e.matmul(ps_d[tt % 2][:, :], lhs,
                                                b_q8[s + 3][:, dt * 512:(dt + 1) * 512],
                                                start=(dt == 0), stop=(dt == ND - 1))
                            return inst
                        w = list(t_wq[P])
                        if g >= 2:
                            w.append(t_h2vw[g - 2][0])
                            w.append(t_h2vw[g - 2][1])
                        t_mmg[P][tt] = sch.add("T", em_mm, w)

                        for j in range(2):
                            w_idx = 2 * P + j
                            psv = (ps_a if j == 0 else ps_c)[tt % 2]
                            psg = (ps_b if j == 0 else ps_d)[tt % 2]
                            sl = b_sl[(2 * g + j) % 4]
                            h2b = b_h2[(2 * g + j) % 3]
                            t_sil = sch.add("S", (lambda sl=sl, psg=psg, tt=tt: (
                                lambda e: e.activation(
                                    sl[:, :], psg[:, :], ACTF.Silu,
                                    scale=v_c1[:, tt:tt + 1])))(), [t_mmg[P][tt]])
                            w2 = [t_sil]
                            if 2 * g + j >= 3:
                                go = 2 * g + j - 3
                                w2.append(t_h2sp[go // 2][go % 2])
                            t_h2vw[g][j] = _seq(sch, "V", [
                                (lambda h2b=h2b, psv=psv, sl=sl, tt=tt: lambda e: e.scalar_tensor_tensor(
                                    h2b[:, :], psv[:, :],
                                    v_c1[:, tt:tt + 1], sl[:, :],
                                    ALU.mult, ALU.mult))(),
                                (lambda h2b=h2b, j=j: lambda e: e.reduce_max(
                                    b_hm[j][:, 0:1], h2b[:, :], axis=AX.X,
                                    apply_absolute_value=True))(),
                                (lambda tt=tt, j=j: lambda e: e.tensor_max(
                                    v_hmax[:, tt:tt + 1], v_hmax[:, tt:tt + 1],
                                    b_hm[j][:, 0:1]))(),
                            ], w2)
                            t_h2sp[g][j] = sch.add("G", (lambda w_idx=w_idx, tt=tt, h2b=h2b: (
                                lambda e: e.dma_start(
                                    d_h2[tt * 128:(tt + 1) * 128,
                                         w_idx * 512:(w_idx + 1) * 512],
                                    h2b[:, :])))(), [t_h2vw[g][j]], dma=True)

                        dispense_q(items)
                        if P < 4:
                            dispense_wos()
                        else:
                            dispense_wo()

                _wire(nc, sch, sems)

        esC = ExitStack()
        with esC:
            b_wq8r = esC.enter_context(nc.sbuf_tensor("b_wq8r", [128, NH * 2048], F8))
            k2T = [esC.enter_context(nc.sbuf_tensor(f"k2T{i}", [128, NH * 128], BF16)) for i in range(2)]
            NH2R = 3
            c_h2i = [esC.enter_context(nc.sbuf_tensor(f"c_h2i{i}", [128, 512], F32)) for i in range(NH2R)]
            c_kt = [esC.enter_context(nc.sbuf_tensor(f"c_kt{i}", [128, 512], I32)) for i in range(2)]
            c_m1 = [esC.enter_context(nc.sbuf_tensor(f"c_m1{i}", [128, 512], BF16)) for i in range(2)]
            c_m2 = esC.enter_context(nc.sbuf_tensor("c_m2", [128, 512], F32))
            NKB = 16
            c_kb = [esC.enter_context(nc.sbuf_tensor(f"c_kb{i}", [128, 512], BF16)) for i in range(NKB)]
            c_x1 = esC.enter_context(nc.sbuf_tensor("c_x1", [128, D], F32))
            c_z2 = esC.enter_context(nc.sbuf_tensor("c_z2", [128, D], F32))
            c_zs = esC.enter_context(nc.sbuf_tensor("c_zs", [128, 8], F32))
            ps_trb2 = [esC.enter_context(nc.psum_tensor(f"ps_trbC{i}", [128, 1024], BF16)) for i in range(2)]
            ps_o = [esC.enter_context(nc.psum_tensor(f"ps_o{i}", [128, 512], F32)) for i in range(6)]

            sch = Sched(prev=nc._schB)

            t_c23 = _seq(sch, "V", [
                lambda e: e.tensor_scalar(v_c2[:, :], v_hmax[:, :], EPS, None, ALU.max),
                lambda e: e.tensor_scalar(v_c3[:, :], v_c2[:, :], 1.0 / QB, v_sout[:, 0:1],
                                          ALU.mult, ALU.mult),
                lambda e: e.reciprocal(v_Mh[:, :], v_c2[:, :]),
                lambda e: e.tensor_scalar(v_c2[:, :], v_Mh[:, :], QB, None, ALU.mult),
            ])
            s3 = d_wq8.rearrange("(a p) o -> p a o", p=128)
            o3 = b_wq8r[:, :].rearrange("p (a o) -> p a o", o=2048)
            t_wq8r_l = []
            for gidx in range(8):
                t_wq8r_l.append(sch.add(
                    ("Q", "R", "G")[gidx % 3],
                    (lambda g=gidx: (lambda e: e.dma_start(
                        o3[:, g * 8:(g + 1) * 8, :], s3[:, g * 8:(g + 1) * 8, :])))(),
                    dma=True))
            t_wq8r = t_wq8r_l

            t_x1cur = [None] * NT
            t_x1cur[0] = sch.add("G", lambda e: e.dma_start(c_x1[:, :], d_x1[0:128, :]), dma=True)

            t_k2e = {}
            t_qc = {}
            t_q4 = {}
            t_ktr2 = {}
            t_ev2 = [[None] * 4 for _ in range(NT)]
            t_mm2 = [[None] * 4 for _ in range(NT)]
            t_st = [None] * NT

            t_h2l = {}
            h2_state = {"n": 0}

            def pump_h2(upto):
                while h2_state["n"] < min(upto, NT * 16):
                    n = h2_state["n"]
                    h2_state["n"] = n + 1
                    w = [t_qc[n - NH2R]] if n >= NH2R else []
                    t_h2l[n] = sch.add("Q", (lambda n=n, b=n % NH2R: (lambda e: e.dma_start(
                        c_h2i[b][:, :], d_h2[(n // 16) * 128:(n // 16 + 1) * 128,
                                             (n % 16) * 512:(n % 16 + 1) * 512])))(),
                        w, dma=True)

            def build_quant(bt):
                for ch in range(16):
                    cg = bt * 16 + ch
                    pump_h2(cg + NH2R)
                    hv = c_h2i[cg % NH2R][:, :]

                    w = [t_h2l[cg], t_c23]
                    if cg >= 2:
                        w.append(t_q4[cg - 2])
                    t_q2 = sch.add("S", (lambda hv=hv, bt=bt, r=cg % 2: lambda e: e.activation(
                        c_kt[r][:, :], hv, ACTF.Copy, scale=v_c2[:, bt:bt + 1]))(), w)

                    w = [t_q2] + ([t_q4[cg - 2]] if cg >= 2 else [])
                    t_qc[cg] = _seq(sch, "V", [
                        (lambda hv=hv, bt=bt, r=cg % 2: lambda e: e.scalar_tensor_tensor(
                            c_m2[:, :], hv, v_c2[:, bt:bt + 1], c_kt[r][:, :],
                            ALU.mult, ALU.subtract))(),
                        (lambda r=cg % 2: lambda e: e.tensor_scalar(
                            c_m1[r][:, :], c_m2[:, :], -1.0, 0.5,
                            ALU.mult, ALU.is_gt))(),
                        (lambda r=cg % 2: lambda e: e.scalar_tensor_tensor(
                            c_m1[r][:, :], c_m2[:, :], 0.5,
                            c_m1[r][:, :], ALU.is_gt, ALU.subtract))(),
                    ], w)
                    w = [t_qc[cg]] + ([t_ktr2[cg - NKB]] if cg >= NKB else [])
                    t_q4[cg] = sch.add("V", (lambda kbb=cg % NKB, r=cg % 2: lambda e: e.tensor_add(
                        c_kb[kbb][:, :], c_kt[r][:, :], c_m1[r][:, :]))(), w)

            def build_transpose(bt, chs=range(16)):
                for ch in chs:
                    cg = bt * 16 + ch

                    def em_t2(e, kbb=cg % NKB, pb=cg % 2):
                        for j in range(4):
                            inst = e.transpose(ps_trb2[pb][:, j * 128:(j + 1) * 128],
                                               c_kb[kbb][:, j * 128:(j + 1) * 128],
                                               c_idb[:, :])
                        return inst
                    w = [t_q4[cg]]
                    if cg >= 2:
                        w.append(t_k2e[cg - 2])
                    t_ktr2[cg] = sch.add("T", em_t2, w)
                    t_k2e[cg] = sch.add("S", (lambda bt=bt, ch=ch, pb=cg % 2: (lambda e: e.copy(
                        k2T[bt % 2][:, ch * 512:(ch + 1) * 512], ps_trb2[pb][:, 0:512])))(),
                        [t_ktr2[cg]])

            build_quant(0)
            build_transpose(0)
            t_sq2o_prev = None

            for tt in range(NT):
                if tt + 1 < NT:
                    build_quant(tt + 1)

                def em_m2(e, tt=tt):
                    for ht in range(NH):
                        lhs = k2T[tt % 2][:, ht * 128:(ht + 1) * 128]
                        for oc in range(4):
                            inst = e.matmul(ps_o[(4 * tt + oc) % 6][:, :], lhs,
                                            b_wq8r[:, ht * 2048 + oc * 512:
                                                   ht * 2048 + (oc + 1) * 512],
                                            start=(ht == 0), stop=(ht == NH - 1))
                    return inst
                w = [t_k2e[tt * 16 + 15]]
                if tt == 0:
                    w.extend(t_wq8r)
                if tt >= 1:
                    w.append(t_ev2[tt - 1][0])
                    w.append(t_ev2[tt - 1][1])
                t_mmall = sch.add("T", em_m2, w)
                if tt + 1 < NT:
                    build_transpose(tt + 1)
                for oc in range(4):
                    t_mm2[tt][oc] = t_mmall

                    def em_z2(e, tt=tt, oc=oc):
                        return e.scalar_tensor_tensor(
                            c_z2[:, oc * 512:(oc + 1) * 512], ps_o[(4 * tt + oc) % 6][:, :],
                            v_c3[:, tt:tt + 1], c_x1[:, oc * 512:(oc + 1) * 512],
                            ALU.mult, ALU.add, accum_out=c_zs[:, oc:oc + 1])
                    w = [t_mmall, t_x1cur[tt]]
                    if tt >= 1 and oc == 0:
                        w.append(t_st[tt - 1])
                    t_ev2[tt][oc] = sch.add("V", em_z2, w)

                    t_sq2o = sch.add("S", (lambda oc=oc, tt=tt: (lambda e: e.activation(
                        k2T[tt % 2][:, 0:512], c_z2[:, oc * 512:(oc + 1) * 512], ACTF.Square,
                        accum_out=c_zs[:, 4 + oc:5 + oc])))(), [t_ev2[tt][oc]],
                        hard_waits=[t_sq2o_prev])
                    t_sq2o_prev = t_sq2o

                t_var2 = _seq(sch, "V", [
                    lambda e: e.reduce_sum(v_ti[:, 0:1], c_zs[:, 0:4], axis=AX.X),
                    lambda e: e.reduce_sum(v_ti[:, 1:2], c_zs[:, 4:8], axis=AX.X),
                    lambda e: e.tensor_scalar(v_ti[:, 2:3], v_ti[:, 0:1], INV_D, None, ALU.mult),
                    lambda e: e.tensor_scalar(v_ti[:, 3:4], v_ti[:, 1:2], INV_D, None, ALU.mult),
                    lambda e: e.tensor_mul(v_ti[:, 4:5], v_ti[:, 2:3], v_ti[:, 2:3]),
                    lambda e: e.tensor_sub(v_ti[:, 3:4], v_ti[:, 3:4], v_ti[:, 4:5]),
                ], [t_sq2o])
                t_sqr = sch.add("S", lambda e: e.activation(
                    v_ti[:, 5:6], v_ti[:, 3:4], ACTF.Sqrt, bias=c_eps[:, 0:1]), [t_var2])

                t_norm = _seq(sch, "V", [
                    lambda e: e.reciprocal(v_ti[:, 6:7], v_ti[:, 5:6]),
                    lambda e: e.tensor_scalar(c_z2[:, :], c_z2[:, :], v_ti[:, 2:3],
                                              v_ti[:, 6:7], ALU.subtract, ALU.mult),
                ], [t_sqr])
                t_st[tt] = sch.add("Q", (lambda tt=tt: (lambda e: e.dma_start(
                    d_out[tt * 128:(tt + 1) * 128, :], c_z2[:, :])))(),
                    [t_norm], dma=True)

                if tt + 1 < NT:
                    t_x1cur[tt + 1] = sch.add("G", (lambda tt=tt: (lambda e: e.dma_start(
                        c_x1[:, :], d_x1[(tt + 1) * 128:(tt + 2) * 128, :])))(),
                        [t_ev2[tt][3]], dma=True)

            _wire(nc, sch, sems)

    return nc


_NC_CACHE = {}


def kernel(L0_attn_x, L0_attn_Wq, L0_attn_bq, L0_attn_Wkv, L0_attn_bkv,
           L0_ffn_norm_w, L0_ffn_w_gv, L0_ffn_w_out):
    x = np.ascontiguousarray(np.asarray(L0_attn_x, dtype=np.float32))
    wkv = np.asarray(L0_attn_Wkv, dtype=np.float32)
    bkv = np.asarray(L0_attn_bkv, dtype=np.float32)
    wgv = np.asarray(L0_ffn_w_gv, dtype=np.float32)
    wout = np.asarray(L0_ffn_w_out, dtype=np.float32)
    nw = np.asarray(L0_ffn_norm_w, dtype=np.float32)

    wgvT = np.ascontiguousarray(wgv.T)
    woutT = np.ascontiguousarray(wout.T)
    wvT = np.ascontiguousarray(wkv[128:256, :].T)
    bv = np.ascontiguousarray(bkv[128:256].reshape(1, 128))
    nwb = np.ascontiguousarray(np.broadcast_to(nw, (128, D)))
    idf = np.eye(128, dtype=np.float32)
    idb = np.eye(128).astype(ml_dtypes.bfloat16)
    ones1 = np.ones((1, 128), dtype=np.float32)
    ones128 = np.ones((128, 1), dtype=np.float32)

    if "nc" not in _NC_CACHE:
        _NC_CACHE["nc"] = build_nc()
    nc = _NC_CACHE["nc"]

    in_maps = []
    for c in range(8):
        in_maps.append({
            "x": np.ascontiguousarray(x[c]),
            "xT": np.ascontiguousarray(
                x[c].reshape(16, 128, 16, 128).transpose(0, 3, 2, 1).reshape(T, D)),
            "wgvT": wgvT, "woutT": woutT, "wvT": wvT, "bv": bv,
            "nwb": nwb, "idf": idf, "idb": idb,
            "ones1": ones1, "ones128": ones128,
            "epsv": np.full((128, 1), EPS, dtype=np.float32),
        })
    res = run_bass_kernel_spmd(nc, in_maps, core_ids=list(range(8)))
    out = np.stack([np.asarray(res.results[c]["out"]) for c in range(8)], axis=0)
    return out.astype(np.float32)


# version 12
# speedup vs baseline: 1.0292x; 1.0086x over previous
import numpy as np
import ml_dtypes
from contextlib import ExitStack

import concourse.bass as bass
import concourse.mybir as mybir
import concourse.bass_utils as _bass_utils
from concourse.bass_utils import run_bass_kernel_spmd

if not getattr(_bass_utils, "_ldwopt_patched", False):
    _orig_run_command = _bass_utils.run_command

    def _run_command_ldwopt(cmd, **kw):
        cmd = ["--enable-ldw-opt=true" if c == "--enable-ldw-opt=false" else c
               for c in cmd]
        return _orig_run_command(cmd, **kw)

    _bass_utils.run_command = _run_command_ldwopt
    _bass_utils._ldwopt_patched = True

F32 = mybir.dt.float32
BF16 = mybir.dt.bfloat16
F8 = mybir.dt.float8e4
I32 = mybir.dt.int32
ALU = mybir.AluOpType
ACTF = mybir.ActivationFunctionType
AX = mybir.AxisListType

T = 2048
D = 2048
H = 8192
O1 = 2 * H
O2 = D
NT = T // 128
ND = D // 128
NH = H // 128
NWP = H // 512
EPS = 1e-5
QB = 127.0
INV_D = 1.0 / D
N_WGV = float(O1 * D)
N_WOUT = float(O2 * H)

ENGS = ("T", "V", "S", "Q", "G")


NQSEM = 8
QUEUES = ("Q", "G", "R")


class _Op:
    __slots__ = ("eng", "emit", "waits", "dma", "ndma", "need_inc", "sem", "val",
                 "prev_val", "hard")

    def __init__(self, eng, emit, waits, dma, ndma):
        self.eng, self.emit, self.waits, self.dma = eng, emit, list(waits), dma
        self.ndma = ndma
        self.need_inc = dma
        self.sem = None
        self.val = None
        self.prev_val = 0


class Sched:

    def __init__(self, prev=None):
        self.ops = []
        self.prev = prev

    def add(self, eng, emit, waits=(), dma=False, ndma=1, hard_waits=()):
        op = _Op(eng, emit, [w for w in waits if w is not None], dma, ndma)
        op.hard = [w for w in hard_waits if w is not None]
        self.ops.append(op)
        return op

    def finalize(self):
        prev = {"V": None, "S": None, "G": None}
        for op in self.ops:
            if op.eng in prev:
                if prev[op.eng] is not None:
                    op.hard.append(prev[op.eng])
                prev[op.eng] = op
        lastc = {}
        for op in self.ops:
            if not op.dma and op.eng in ("T", "V", "S", "G"):
                lastc[op.eng] = op
        for op in lastc.values():
            op.need_inc = True
        for op in self.ops:
            for w in op.waits:
                if w.eng != op.eng or w.dma:
                    w.need_inc = True
            for w in op.hard:
                w.need_inc = True
        if self.prev is not None:
            cnt = dict(self.prev.final_cnt)
            qrot = dict(self.prev.final_qrot)
            qcnt = dict(self.prev.qtot)
            self.entry_waits = dict(cnt)
            for k, v in qcnt.items():
                self.entry_waits[k] = v
        else:
            cnt = {e: 0 for e in ("T", "V", "S", "G")}
            qrot = {q: 0 for q in QUEUES}
            qcnt = {(q, i): 0 for q in QUEUES for i in range(NQSEM)}
            self.entry_waits = {}
        self.qtot = qcnt
        for op in self.ops:
            if op.dma:
                i = qrot[op.eng]
                qrot[op.eng] = (i + 1) % NQSEM
                op.sem = (op.eng, i)
                op.prev_val = qcnt[op.sem]
                qcnt[op.sem] += 16 * op.ndma
                op.val = qcnt[op.sem]
            elif op.need_inc:
                cnt[op.eng] += 1
                op.sem = op.eng
                op.val = cnt[op.eng]
        for op in self.ops:
            ws = {}
            for w in op.waits:
                if not w.need_inc:
                    continue
                if (not w.dma) and w.eng == op.eng:
                    continue
                ws[w.sem] = max(ws.get(w.sem, 0), w.val)
            for w in op.hard:
                ws[w.sem] = max(ws.get(w.sem, 0), w.val)
            op.waits = ws
        self.final_cnt = cnt
        self.final_qrot = qrot

    def replay(self, eng_name, eng, sems):
        names = ("S", "R") if eng_name == "S" else (eng_name,)
        last = {}
        for sk, v in self.entry_waits.items():
            if v > 0 and sk not in names:
                eng.wait_ge(sems[sk], v)
                last[sk] = v
            else:
                last[sk] = v
        for op in self.ops:
            if op.eng not in names:
                continue
            if op.dma and op.prev_val > 0 and last.get(op.sem, 0) < op.prev_val:
                eng.wait_ge(sems[op.sem], op.prev_val)
                last[op.sem] = op.prev_val
            for sk, v in op.waits.items():
                if v > last.get(sk, 0):
                    eng.wait_ge(sems[sk], v)
                    last[sk] = v
            out = op.emit(eng)
            if op.dma:
                insts = out if isinstance(out, list) else [out]
                assert len(insts) == op.ndma, (op.eng, len(insts), op.ndma)
                for inst in insts:
                    inst.then_inc(sems[op.sem], 16)
            elif op.need_inc:
                out.then_inc(sems[op.sem], 1)
        for qn in names:
            if qn in QUEUES:
                for i in range(NQSEM):
                    tot = self.qtot[(qn, i)]
                    if tot > last.get((qn, i), 0):
                        eng.wait_ge(sems[(qn, i)], tot)


def _seq(sch, eng, emits, waits=()):
    t = None
    for i, em in enumerate(emits):
        t = sch.add(eng, em, waits if i == 0 else ())
    return t


def _wire(nc, sched, sems):
    sched.finalize()
    with nc.Block() as blk:
        blk.tensor(lambda e: sched.replay("T", e, sems))
        blk.vector(lambda e: sched.replay("V", e, sems))
        blk.scalar(lambda e: sched.replay("S", e, sems))
        blk.sync(lambda e: sched.replay("Q", e, sems))
        blk.gpsimd(lambda e: sched.replay("G", e, sems))


def build_nc():
    nc = bass.Bass("TRN2", target_bir_lowering=False)

    d_x = nc.dram_tensor("x", [T, D], F32, kind="ExternalInput").ap()
    d_xT = nc.dram_tensor("xT", [D, T], F32, kind="ExternalInput").ap()
    d_wgvT = nc.dram_tensor("wgvT", [D, O1], F32, kind="ExternalInput").ap()
    d_woutT = nc.dram_tensor("woutT", [H, O2], F32, kind="ExternalInput").ap()
    d_wvT = nc.dram_tensor("wvT", [D, 128], F32, kind="ExternalInput").ap()
    d_bv = nc.dram_tensor("bv", [1, 128], F32, kind="ExternalInput").ap()
    d_nwb = nc.dram_tensor("nwb", [128, D], F32, kind="ExternalInput").ap()
    d_idf = nc.dram_tensor("idf", [128, 128], F32, kind="ExternalInput").ap()
    d_idb = nc.dram_tensor("idb", [128, 128], BF16, kind="ExternalInput").ap()
    d_ones1 = nc.dram_tensor("ones1", [1, 128], F32, kind="ExternalInput").ap()
    d_ones128 = nc.dram_tensor("ones128", [128, 1], F32, kind="ExternalInput").ap()
    d_epsv = nc.dram_tensor("epsv", [128, 1], F32, kind="ExternalInput").ap()
    d_out = nc.dram_tensor("out", [T, D], F32, kind="ExternalOutput").ap()
    d_h2 = nc.dram_tensor("h2sp", [T, H], F32, kind="Internal").ap()
    d_x1 = nc.dram_tensor("x1sp", [T, D], F32, kind="Internal").ap()
    d_wq8 = nc.dram_tensor("wq8sp", [H, O2], F8, kind="Internal").ap()
    d_sc1 = nc.dram_tensor("sc1sp", [1, 1], F32, kind="Internal").ap()

    es = ExitStack()
    with es:
        c_idb = es.enter_context(nc.sbuf_tensor("c_idb", [128, 128], BF16))
        c_ones1 = es.enter_context(nc.sbuf_tensor("c_ones1", [1, 128], F32))
        c_ones128 = es.enter_context(nc.sbuf_tensor("c_ones128", [128, 1], F32))
        c_bv = es.enter_context(nc.sbuf_tensor("c_bv", [1, 128], F32))
        c_eps = es.enter_context(nc.sbuf_tensor("c_eps", [128, 1], F32))
        v_Mh = es.enter_context(nc.sbuf_tensor("v_Mh", [128, NT], F32))
        v_c1 = es.enter_context(nc.sbuf_tensor("v_c1", [128, NT], F32))
        v_hmax = es.enter_context(nc.sbuf_tensor("v_hmax", [128, NT], F32))
        v_c2 = es.enter_context(nc.sbuf_tensor("v_c2", [128, NT], F32))
        v_c3 = es.enter_context(nc.sbuf_tensor("v_c3", [128, NT], F32))
        v_sgv = es.enter_context(nc.sbuf_tensor("v_sgv", [128, 1], F32))
        v_hsgv = es.enter_context(nc.sbuf_tensor("v_hsgv", [128, 1], F32))
        v_sout = es.enter_context(nc.sbuf_tensor("v_sout", [128, 1], F32))
        v_hsout = es.enter_context(nc.sbuf_tensor("v_hsout", [128, 1], F32))
        v_nhsout = es.enter_context(nc.sbuf_tensor("v_nhsout", [128, 1], F32))
        v_ti = es.enter_context(nc.sbuf_tensor("v_ti", [128, 8], F32))


        sems = {e: es.enter_context(nc.semaphore(f"sem_{e}")) for e in ("T", "V", "S", "G")}
        for qe in QUEUES:
            for i in range(NQSEM):
                sems[(qe, i)] = es.enter_context(nc.semaphore(f"sem_{qe}{i}"))

        esAB = ExitStack()
        with esAB:
            k1T = esAB.enter_context(nc.sbuf_tensor("k1T", [128, ND * T], BF16))
            b_wo8 = [esAB.enter_context(nc.sbuf_tensor(f"b_wo8{i}", [128, 2048], F8)) for i in range(1)]


            esA = ExitStack()
            with esA:
                c_wvT = esA.enter_context(nc.sbuf_tensor("c_wvT", [128, ND * 128], F32))
                c_nwb = esA.enter_context(nc.sbuf_tensor("c_nwb", [128, D], F32))
                a_x = [esA.enter_context(nc.sbuf_tensor(f"a_x{i}", [128, D], F32)) for i in range(3)]
                a_xT = esA.enter_context(nc.sbuf_tensor("a_xT", [128, ND * 128], F32))
                a_z = esA.enter_context(nc.sbuf_tensor("a_z", [128, D], F32))
                a_u = esA.enter_context(nc.sbuf_tensor("a_u", [128, D], F32))
                a_kt = esA.enter_context(nc.sbuf_tensor("a_kt", [128, D], I32))
                a_m1 = esA.enter_context(nc.sbuf_tensor("a_m1", [128, D], F32))
                a_m2 = esA.enter_context(nc.sbuf_tensor("a_m2", [128, D], F32))
                a_kb = esA.enter_context(nc.sbuf_tensor("a_kb", [128, D], BF16))
                a_gvB = esA.enter_context(nc.sbuf_tensor("a_gvB", [128, 4 * D], F32))
                a_y2 = esA.enter_context(nc.sbuf_tensor("a_y2", [128, D], F32))
                a_pgv = esA.enter_context(nc.sbuf_tensor("a_pgv", [128, 64], F32))
                st = esA.enter_context(nc.sbuf_tensor("a_st", [128, 16], F32))
                ps_v = esA.enter_context(nc.psum_tensor("ps_v", [128, 512], F32))
                ps_s = esA.enter_context(nc.psum_tensor("ps_s", [128, 512], F32))
                ps_tr = [esA.enter_context(nc.psum_tensor(f"ps_tr{i}", [128, 512], F32)) for i in range(2)]
                ps_trb = [esA.enter_context(nc.psum_tensor(f"ps_trbA{i}", [128, 1024], BF16)) for i in range(2)]

                sch = Sched()
                nc._schA = sch

                t_pre_all = []
                for dst, src in [
                    (c_idb[:, :], d_idb),
                    (c_ones1[:, :], d_ones1), (c_ones128[:, :], d_ones128),
                    (c_bv[:, :], d_bv), (c_nwb[:, :], d_nwb),
                    (c_eps[:, :], d_epsv),
                    (c_wvT[:, :].rearrange("p (a j) -> p a j", j=128),
                     d_wvT.rearrange("(a p) j -> p a j", p=128)),
                ]:
                    t_pre_all.append(sch.add("Q", (lambda dd, ss: (lambda e: e.dma_start(dd, ss)))(dst, src), dma=True))
                t_pre = t_pre_all[-1]

                NSG = 128
                t_gvl = [[None, None] for _ in range(NSG // 2)]
                t_gvp = [None] * (NSG // 2)
                t_x = [None] * NT
                t_z = [None] * NT
                t_u = [None] * NT
                t_ysp = [None] * NT
                t_qk = [None] * NT
                t_kb = [None] * NT
                t_k1e = [[None, None] for _ in range(NT)]
                t_xTl = [None] * NT
                t_vmmA = [None] * NT
                t_ktr = [[None, None] for _ in range(NT)]
                t_cact = [None] * NT

                def emit_sload(p):
                    t = 2 * p
                    rb, cs = t // 8, t % 8
                    s0 = t % 4
                    w = [t_gvp[p - 2]] if p >= 2 else list(t_pre_all)
                    qn = ("Q", "R", "G")[p % 3]
                    srcp = d_wgvT[rb * 128:(rb + 1) * 128, cs * 2048:(cs + 2) * 2048]
                    t_gvl[p][0] = sch.add(qn, (lambda s0=s0, ss=srcp: (lambda e: e.dma_start(
                        a_gvB[:, s0 * 2048:(s0 + 2) * 2048], ss)))(),
                        w, dma=True)

                def emit_sabs(p):
                    s0 = (2 * p) % 4
                    t_gvp[p] = sch.add("S", (lambda s0=s0, p=p: (lambda e: e.activation(
                        a_gvB[:, s0 * 2048:s0 * 2048 + 4096],
                        a_gvB[:, s0 * 2048:s0 * 2048 + 4096], ACTF.Abs,
                        accum_out=a_pgv[:, p:p + 1])))(), [t_gvl[p][0]])

                def emit_xload(tt):
                    w = [t_zsq[tt - 3]] if tt >= 3 else list(t_pre_all)
                    t_x[tt] = sch.add("Q", (lambda b, r0: (lambda e: e.dma_start(
                        a_x[b][:, :], d_x[r0:r0 + 128, :])))(tt % 3, tt * 128), w, dma=True)

                def emit_xT(tt):
                    w = [t_vmmA[tt - 1]] if tt >= 1 else list(t_pre_all)
                    t_xTl[tt] = sch.add("Q", (lambda r0=tt * 128: (lambda e: e.dma_start(
                        a_xT[:, :], d_xT[r0:r0 + 128, :])))(),
                        w, dma=True)

                def emit_vmm(tt):
                    def em_v(e):
                        for dt in range(ND):
                            e.matmul(ps_v[:, 0:128],
                                     a_xT[:, dt * 128:(dt + 1) * 128],
                                     c_wvT[:, dt * 128:(dt + 1) * 128],
                                     start=(dt == 0), stop=False)
                        return e.matmul(ps_v[:, 0:128], c_ones1[0:1, :], c_bv[0:1, :],
                                        start=False, stop=True)
                    w = [t_xTl[tt]] + ([t_z[tt - 1]] if tt >= 1 else [])
                    t_vmmA[tt] = sch.add("T", em_v, w)
                    return t_vmmA[tt]

                def emit_z(tt, t_vmm):
                    def em_z(e, b=tt % 3):
                        vbc = ps_v[:, 0:128].unsqueeze(1).broadcast_to((128, ND, 128))
                        return e.scalar_tensor_tensor(a_z[:, :], a_x[b][:, :], 1.0, vbc,
                                                      ALU.mult, ALU.add,
                                                      accum_out=st[:, 0:1])
                    w = [t_vmm, t_x[tt]] + ([t_zsq[tt - 1]] if tt >= 1 else [])
                    t_z[tt] = sch.add("V", em_z, w)
                    t_zsq[tt] = sch.add("S", (lambda tt=tt: lambda e: e.activation(
                        a_x[tt % 3][:, :], a_z[:, :], ACTF.Square,
                        accum_out=st[:, 1:2]))(), [t_z[tt]])

                def emit_var(tt):
                    t_var = _seq(sch, "V", [
                        lambda e: e.tensor_scalar(st[:, 4:5], st[:, 0:1], INV_D, None, ALU.mult),
                        lambda e: e.tensor_scalar(st[:, 2:3], st[:, 1:2], INV_D, None, ALU.mult),
                        lambda e: e.tensor_mul(st[:, 12:13], st[:, 4:5], st[:, 4:5]),
                        lambda e: e.tensor_sub(st[:, 2:3], st[:, 2:3], st[:, 12:13]),
                    ], [t_zsq[tt]])
                    t_sq1 = sch.add("S", lambda e: e.activation(
                        st[:, 3:4], st[:, 2:3], ACTF.Sqrt, bias=c_eps[:, 0:1]), [t_var])
                    t_y[tt] = sch.add("V", lambda e: e.reciprocal(
                        st[:, 8:9], st[:, 3:4]), [t_sq1])

                def emit_iter(tt):
                    nxt = tt + 1 if tt + 1 < NT else None
                    if nxt is not None:
                        emit_xT(nxt)
                        emit_xload(nxt)
                        emit_sload(nxt * 4 + 0)
                        emit_sload(nxt * 4 + 1)

                    w = [t_y[tt]] + ([t_qk[tt - 1]] if tt >= 1 else [])
                    t_p = _seq(sch, "V", [
                        lambda e: e.scalar_tensor_tensor(a_u[:, :], a_z[:, :],
                                                         st[:, 4:5], c_nwb[:, :],
                                                         ALU.subtract, ALU.mult),
                        lambda e: e.reduce_max(st[:, 10:11], a_u[:, :], axis=AX.X,
                                               apply_absolute_value=True),
                    ], w)
                    w = [t_ysp[tt - 1]] if tt >= 1 else []
                    t_y2 = _seq(sch, "V", [
                        lambda e: e.tensor_scalar(st[:, 5:6], st[:, 8:9], 2.0, None,
                                                  ALU.mult),
                        lambda e: e.tensor_scalar(a_y2[:, :], a_z[:, :], st[:, 4:5],
                                                  st[:, 5:6], ALU.subtract, ALU.mult),
                    ], w)
                    t_ysp[tt] = sch.add("G", (lambda r0: (lambda e: e.dma_start(
                        d_x1[r0:r0 + 128, :], a_y2[:, :])))(tt * 128),
                        [t_y2], dma=True)
                    t_c1 = _seq(sch, "V", [
                        lambda e: e.tensor_mul(st[:, 6:7], st[:, 8:9], st[:, 8:9]),
                        lambda e: e.tensor_mul(st[:, 6:7], st[:, 6:7], st[:, 2:3]),
                    ], [t_p])
                    t_sq2 = sch.add("S", lambda e: e.activation(
                        st[:, 7:8], st[:, 6:7], ACTF.Sqrt, bias=c_eps[:, 0:1]),
                        [t_c1])
                    t_cact[tt] = _seq(sch, "V", [
                        lambda e: e.reciprocal(st[:, 11:12], st[:, 7:8]),
                        lambda e: e.tensor_mul(st[:, 12:13], st[:, 8:9], st[:, 11:12]),
                        lambda e: e.tensor_mul(st[:, 14:15], st[:, 12:13], st[:, 10:11]),
                        (lambda tt=tt: lambda e: e.tensor_scalar(
                            v_Mh[:, tt:tt + 1], st[:, 14:15], EPS, None, ALU.max))(),
                        (lambda tt=tt: lambda e: e.reciprocal(
                            st[:, 13:14], v_Mh[:, tt:tt + 1]))(),
                        lambda e: e.tensor_scalar(st[:, 9:10], st[:, 13:14], QB,
                                                  st[:, 12:13], ALU.mult, ALU.mult),
                    ], [t_sq2])
                    w = [t_cact[tt]] + ([t_kb[tt - 1]] if tt >= 1 else [])
                    t_qk[tt] = sch.add("S", lambda e: e.activation(
                        a_kt[:, :], a_u[:, :], ACTF.Copy, scale=st[:, 9:10]), w)
                    if nxt is not None:
                        emit_sabs(nxt * 4 + 0)
                        emit_sabs(nxt * 4 + 1)
                        emit_sload(nxt * 4 + 2)
                        emit_sload(nxt * 4 + 3)
                        t_vmm = emit_vmm(nxt)
                        emit_z(nxt, t_vmm)
                    w = [t_qk[tt]] + ([t_ktr[tt - 1][1]] if tt >= 1 else [])
                    t_kb[tt] = _seq(sch, "V", [
                        lambda e: e.scalar_tensor_tensor(a_m2[:, :], a_u[:, :],
                                                         st[:, 9:10], a_kt[:, :],
                                                         ALU.mult, ALU.subtract),
                        lambda e: e.tensor_scalar(a_m1[:, :], a_m2[:, :], -1.0, 0.5,
                                                  ALU.mult, ALU.is_gt),
                        lambda e: e.scalar_tensor_tensor(a_m1[:, :], a_m2[:, :], 0.5,
                                                         a_m1[:, :], ALU.is_gt,
                                                         ALU.subtract),
                        lambda e: e.tensor_add(a_kb[:, :], a_kt[:, :], a_m1[:, :]),
                    ], w)
                    if nxt is not None:
                        emit_var(nxt)
                        emit_sabs(nxt * 4 + 2)
                        emit_sabs(nxt * 4 + 3)
                    for g in range(2):
                        def em_kt2(e, g=g):
                            for j in range(8):
                                dt = g * 8 + j
                                inst = e.transpose(ps_trb[g][:, j * 128:(j + 1) * 128],
                                                   a_kb[:, dt * 128:(dt + 1) * 128],
                                                   c_idb[:, :])
                            return inst
                        w = [t_kb[tt]]
                        if tt >= 1:
                            w.append(t_k1e[tt - 1][g])
                        t_ktr[tt][g] = sch.add("T", em_kt2, w)

                        def em_ke(e, g=g, tt=tt):
                            out_ap = bass.AP(k1T, (g * 8) * T + tt * 128,
                                             [[ND * T, 128], [T, 8], [1, 128]])
                            return e.tensor_copy(out_ap, ps_trb[g][:, :])
                        t_k1e[tt][g] = sch.add("V", em_ke, [t_ktr[tt][g]])

                t_zsq = [None] * NT
                t_y = [None] * NT
                emit_xT(0)
                emit_xload(0)
                emit_z(0, emit_vmm(0))
                emit_var(0)
                emit_sload(0)
                emit_sload(1)
                emit_sabs(0)
                emit_sabs(1)
                emit_sload(2)
                emit_sload(3)
                emit_sabs(2)
                emit_sabs(3)
                for tt in range(NT):
                    emit_iter(tt)

                t_r1 = sch.add("V", lambda e: e.reduce_sum(
                    v_ti[:, 0:1], a_pgv[:, :], axis=AX.X), [t_gvp[NSG // 2 - 1]])
                t_r2 = sch.add("T", lambda e: e.matmul(
                    ps_s[0:1, 0:1], v_ti[:, 0:1], c_ones128[:, 0:1],
                    start=True, stop=True), [t_r1, t_z[NT - 1]])
                t_r3 = sch.add("V", lambda e: e.tensor_copy(v_ti[0:1, 4:5], ps_s[0:1, 0:1]),
                               [t_r2])
                t_r4 = sch.add("T", lambda e: e.matmul(
                    ps_v[:, 0:1], c_ones1[0:1, :], v_ti[0:1, 4:5],
                    start=True, stop=True), [t_r3])

                _seq(sch, "V", [
                    lambda e: e.tensor_scalar(v_sgv[:, 0:1], ps_v[:, 0:1], 1.0 / N_WGV, EPS,
                                              ALU.mult, ALU.max),
                    lambda e: e.tensor_scalar(v_hsgv[:, 0:1], v_sgv[:, 0:1], 0.5, None, ALU.mult),
                    lambda e: e.tensor_scalar(v_c1[:, :], v_Mh[:, :], 1.0 / QB, v_sgv[:, 0:1],
                                              ALU.mult, ALU.mult),
                    lambda e: e.memset(v_hmax[:, :], 0.0),
                ], [t_r4, t_cact[NT - 1]])

                _wire(nc, sch, sems)

            esB = ExitStack()
            with esB:
                b_wq = [esB.enter_context(nc.sbuf_tensor(f"b_wq{i}", [128, 8 * 512], F32)) for i in range(2)]
                b_q8 = [esB.enter_context(nc.sbuf_tensor(f"b_q8{i}", [128, ND * 512], F8)) for i in range(8)]
                b_h2 = [esB.enter_context(nc.sbuf_tensor(f"b_h2{i}", [128, 512], F32)) for i in range(3)]
                b_sl = [esB.enter_context(nc.sbuf_tensor(f"b_sl{i}", [128, 512], F32)) for i in range(4)]
                b_wo = [esB.enter_context(nc.sbuf_tensor(f"b_wo{i}", [128, 2048], F32)) for i in range(3)]
                b_po = esB.enter_context(nc.sbuf_tensor("b_po", [128, 64], F32))
                b_wm = esB.enter_context(nc.sbuf_tensor("b_wm", [128, 2048], BF16))
                b_hm = [esB.enter_context(nc.sbuf_tensor(f"b_hm{i}", [128, 1], F32)) for i in range(2)]
                ps_a = [esB.enter_context(nc.psum_tensor(f"ps_a{i}", [128, 512], F32)) for i in range(2)]
                ps_b = [esB.enter_context(nc.psum_tensor(f"ps_b{i}", [128, 512], F32)) for i in range(2)]
                ps_c = [esB.enter_context(nc.psum_tensor(f"ps_c{i}", [128, 512], F32)) for i in range(2)]
                ps_d = [esB.enter_context(nc.psum_tensor(f"ps_d{i}", [128, 512], F32)) for i in range(2)]

                sch = Sched(prev=nc._schA)
                nc._schB = sch

                NP = NWP // 2
                t_wq = [[None] * 4 for _ in range(NP)]
                t_mmg = [[None] * NT for _ in range(NP)]
                t_h2vw = [[None, None] for _ in range(NP * NT)]
                t_h2sp = [[None, None] for _ in range(NP * NT)]
                t_woab = [None] * 64
                t_woq = [None] * 64
                t_woqo = [None] * 64
                t_sout = None

                wgvT_3d = d_wgvT.rearrange("(a p) o -> p a o", p=128)
                qctr = [0]
                t_qld = {}
                t_q8q = {}

                def emit_qload(qg):
                    P, rr = divmod(qg, 8)
                    h, qp = divmod(rr, 2)
                    w_idx = 2 * P + (h % 2)
                    col0 = (h // 2) * H + w_idx * 512
                    w = [t_q8q[qg - 2]] if qg >= 2 else []
                    t_qld[qg] = sch.add("Q", (lambda b=qg % 2, col0=col0, qp=qp: (
                        lambda e: e.dma_start(
                            b_wq[b][:, :].rearrange("p (a j) -> p a j", j=512),
                            wgvT_3d[:, qp * 8:(qp + 1) * 8, col0:col0 + 512])))(),
                        w, dma=True)

                def pair_quant_items(P):
                    items = []
                    for h in range(4):
                        qslot = (P % 2) * 4 + (h % 2) + 2 * (h // 2)
                        for qp in range(2):
                            qg = qctr[0]
                            qctr[0] += 1
                            for c2 in range(2):
                                items.append((qg, h, qslot, qp, c2, P))
                    return items

                def dispense_q(items):
                    for _ in range(2):
                        if not items:
                            return
                        qg, h, qslot, qp, c2, P = items.pop(0)
                        if c2 == 0:
                            if qg not in t_qld:
                                emit_qload(qg)
                            if qg + 1 < 8 * NP and (qg + 1) not in t_qld:
                                emit_qload(qg + 1)
                        qbuf = b_q8[qslot]
                        w = []
                        if c2 == 0:
                            w = [t_qld[qg]]
                            if P >= 2 and qp == 0:
                                w.append(t_mmg[P - 2][NT - 1])
                        sch.add("V", (lambda b=qg % 2, c2=c2: (lambda e: e.tensor_scalar(
                            b_wm[:, :], b_wq[b][:, c2 * 2048:(c2 + 1) * 2048],
                            -1.0, v_hsgv[:, 0:1], ALU.mult, ALU.is_gt)))(), w)
                        tok = sch.add("V", (lambda b=qg % 2, qbuf=qbuf, qp=qp, c2=c2: (
                            lambda e: e.scalar_tensor_tensor(
                                qbuf[:, qp * 4096 + c2 * 2048:qp * 4096 + (c2 + 1) * 2048],
                                b_wq[b][:, c2 * 2048:(c2 + 1) * 2048],
                                v_hsgv[:, 0:1], b_wm[:, :], ALU.is_gt, ALU.subtract)))(), [])
                        if c2 == 1:
                            t_q8q[qg] = tok
                            if qp == 1:
                                t_wq[P][h] = tok

                wo_queue = []
                wo_state = {"next_load": 0, "next_q": 0, "loads": {}}

                def _emit_wo_load(tw):
                    w = ([t_woq[tw - 3]] if tw >= 3 else []) + [t_sout]
                    wo_state["loads"][tw] = sch.add(
                        "R", (lambda r0, b: (lambda e: e.dma_start(
                            b_wo[b][:, :], d_woutT[r0:r0 + 128, :])))(tw * 128, tw % 3),
                        w, dma=True)

                def dispense_wo():
                    i = wo_state["next_q"]
                    if i >= 64 or t_sout is None:
                        return
                    while wo_state["next_load"] < min(i + 3, 64):
                        _emit_wo_load(wo_state["next_load"])
                        wo_state["next_load"] += 1
                    tw = i
                    wo_state["next_q"] += 1
                    w2 = [wo_state["loads"][tw]] + ([t_woqo[tw - 1]] if tw >= 1 else [])
                    sch.add("V", (lambda b=tw % 3: lambda e: e.tensor_scalar(
                        b_wm[:, :], b_wo[b][:, :], -1.0,
                        v_hsout[:, 0:1], ALU.mult, ALU.is_gt))(), w2)
                    t_woq[tw] = sch.add("V", (lambda b=tw % 3, b8=0: lambda e: e.scalar_tensor_tensor(
                        b_wo8[b8][:, :], b_wo[b][:, :],
                        v_hsout[:, 0:1], b_wm[:, :],
                        ALU.is_gt, ALU.subtract))(), [])
                    t_woqo[tw] = sch.add("G", (lambda r0, b: (lambda e: e.dma_start(
                        d_wq8[r0:r0 + 128, :], b_wo8[b][:, :])))(tw * 128, 0),
                        [t_woq[tw]], dma=True)

                wos_state = {"next_load": 0, "next_abs": 0, "loads": {}}

                def dispense_wos():
                    i = wos_state["next_abs"]
                    if i >= 64:
                        return
                    while wos_state["next_load"] < min(i + 3, 64):
                        tw = wos_state["next_load"]
                        w = [t_woab[tw - 3]] if tw >= 3 else []
                        wos_state["loads"][tw] = sch.add(
                            "R", (lambda r0, b: (lambda e: e.dma_start(
                                b_wo[b][:, :], d_woutT[r0:r0 + 128, :])))(
                                tw * 128, tw % 3), w, dma=True)
                        wos_state["next_load"] += 1
                    wos_state["next_abs"] += 1
                    t_woab[i] = sch.add("S", (lambda b, tc: (lambda e: e.activation(
                        b_wo[b][:, :], b_wo[b][:, :], ACTF.Abs,
                        accum_out=b_po[:, tc:tc + 1])))(i % 3, i),
                        [wos_state["loads"][i]])

                items0 = pair_quant_items(0)
                while items0:
                    dispense_q(items0)
                for P in range(NP):
                    items = pair_quant_items(P + 1) if P + 1 < NP else []

                    if P == 4:
                        t_wr1 = sch.add("V", lambda e: e.reduce_sum(
                            v_ti[:, 1:2], b_po[:, :], axis=AX.X), [t_woab[63]])
                        t_wr2 = sch.add("G", lambda e: e.reduce_sum(
                            v_ti[0:1, 5:6], v_ti[:, 1:2], axis=AX.C), [t_wr1])
                        t_wr3 = sch.add("Q", lambda e: e.dma_start(
                            d_sc1, v_ti[0:1, 5:6]), [t_wr2], dma=True)
                        t_wr4 = sch.add("Q", lambda e: e.dma_start(
                            v_ti[:, 7:8], bass.AP(d_sc1.tensor, 0, [[0, 128], [1, 1]])),
                            [t_wr3], dma=True)
                        t_sout = _seq(sch, "V", [
                            lambda e: e.tensor_scalar(v_sout[:, 0:1], v_ti[:, 7:8], 1.0 / N_WOUT,
                                                      EPS, ALU.mult, ALU.max),
                            lambda e: e.tensor_scalar(v_hsout[:, 0:1], v_sout[:, 0:1], 0.5,
                                                      None, ALU.mult),
                            lambda e: e.tensor_scalar(v_nhsout[:, 0:1], v_hsout[:, 0:1], -1.0,
                                                      None, ALU.mult),
                        ], [t_wr4])

                    for tt in range(NT):
                        g = P * NT + tt

                        def em_mm(e, P=P, tt=tt):
                            s = (P % 2) * 4
                            for dt in range(ND):
                                lhs = bass.AP(k1T, dt * T + tt * 128,
                                              [[ND * T, 128], [1, 128]])
                                e.matmul(ps_a[tt % 2][:, :], lhs,
                                         b_q8[s + 0][:, dt * 512:(dt + 1) * 512],
                                         start=(dt == 0), stop=(dt == ND - 1))
                                e.matmul(ps_c[tt % 2][:, :], lhs,
                                         b_q8[s + 1][:, dt * 512:(dt + 1) * 512],
                                         start=(dt == 0), stop=(dt == ND - 1))
                                e.matmul(ps_b[tt % 2][:, :], lhs,
                                         b_q8[s + 2][:, dt * 512:(dt + 1) * 512],
                                         start=(dt == 0), stop=(dt == ND - 1))
                                inst = e.matmul(ps_d[tt % 2][:, :], lhs,
                                                b_q8[s + 3][:, dt * 512:(dt + 1) * 512],
                                                start=(dt == 0), stop=(dt == ND - 1))
                            return inst
                        w = list(t_wq[P])
                        if g >= 2:
                            w.append(t_h2vw[g - 2][0])
                            w.append(t_h2vw[g - 2][1])
                        t_mmg[P][tt] = sch.add("T", em_mm, w)

                        for j in range(2):
                            w_idx = 2 * P + j
                            psv = (ps_a if j == 0 else ps_c)[tt % 2]
                            psg = (ps_b if j == 0 else ps_d)[tt % 2]
                            sl = b_sl[(2 * g + j) % 4]
                            h2b = b_h2[(2 * g + j) % 3]
                            t_sil = sch.add("S", (lambda sl=sl, psg=psg, tt=tt: (
                                lambda e: e.activation(
                                    sl[:, :], psg[:, :], ACTF.Silu,
                                    scale=v_c1[:, tt:tt + 1])))(), [t_mmg[P][tt]])
                            w2 = [t_sil]
                            if 2 * g + j >= 3:
                                go = 2 * g + j - 3
                                w2.append(t_h2sp[go // 2][go % 2])
                            t_h2vw[g][j] = _seq(sch, "V", [
                                (lambda h2b=h2b, psv=psv, sl=sl, tt=tt: lambda e: e.scalar_tensor_tensor(
                                    h2b[:, :], psv[:, :],
                                    v_c1[:, tt:tt + 1], sl[:, :],
                                    ALU.mult, ALU.mult))(),
                                (lambda h2b=h2b, j=j: lambda e: e.reduce_max(
                                    b_hm[j][:, 0:1], h2b[:, :], axis=AX.X,
                                    apply_absolute_value=True))(),
                                (lambda tt=tt, j=j: lambda e: e.tensor_max(
                                    v_hmax[:, tt:tt + 1], v_hmax[:, tt:tt + 1],
                                    b_hm[j][:, 0:1]))(),
                            ], w2)
                            t_h2sp[g][j] = sch.add("G", (lambda w_idx=w_idx, tt=tt, h2b=h2b: (
                                lambda e: e.dma_start(
                                    d_h2[tt * 128:(tt + 1) * 128,
                                         w_idx * 512:(w_idx + 1) * 512],
                                    h2b[:, :])))(), [t_h2vw[g][j]], dma=True)

                        dispense_q(items)
                        if P < 4:
                            dispense_wos()
                        else:
                            dispense_wo()

                _wire(nc, sch, sems)

        esC = ExitStack()
        with esC:
            b_wq8r = esC.enter_context(nc.sbuf_tensor("b_wq8r", [128, NH * 2048], F8))
            k2T = [esC.enter_context(nc.sbuf_tensor(f"k2T{i}", [128, NH * 128], BF16)) for i in range(2)]
            NH2R = 3
            c_h2i = [esC.enter_context(nc.sbuf_tensor(f"c_h2i{i}", [128, 512], F32)) for i in range(NH2R)]
            c_kt = [esC.enter_context(nc.sbuf_tensor(f"c_kt{i}", [128, 512], I32)) for i in range(2)]
            c_m1 = [esC.enter_context(nc.sbuf_tensor(f"c_m1{i}", [128, 512], BF16)) for i in range(2)]
            c_m2 = esC.enter_context(nc.sbuf_tensor("c_m2", [128, 512], F32))
            NKB = 16
            c_kb = [esC.enter_context(nc.sbuf_tensor(f"c_kb{i}", [128, 512], BF16)) for i in range(NKB)]
            c_x1 = esC.enter_context(nc.sbuf_tensor("c_x1", [128, D], F32))
            c_z2 = esC.enter_context(nc.sbuf_tensor("c_z2", [128, D], F32))
            c_zs = esC.enter_context(nc.sbuf_tensor("c_zs", [128, 8], F32))
            ps_trb2 = [esC.enter_context(nc.psum_tensor(f"ps_trbC{i}", [128, 1024], BF16)) for i in range(2)]
            ps_o = [esC.enter_context(nc.psum_tensor(f"ps_o{i}", [128, 512], F32)) for i in range(6)]

            sch = Sched(prev=nc._schB)

            t_c23 = _seq(sch, "V", [
                lambda e: e.tensor_scalar(v_c2[:, :], v_hmax[:, :], EPS, None, ALU.max),
                lambda e: e.tensor_scalar(v_c3[:, :], v_c2[:, :], 1.0 / QB, v_sout[:, 0:1],
                                          ALU.mult, ALU.mult),
                lambda e: e.reciprocal(v_Mh[:, :], v_c2[:, :]),
                lambda e: e.tensor_scalar(v_c2[:, :], v_Mh[:, :], QB, None, ALU.mult),
            ])
            s3 = d_wq8.rearrange("(a p) o -> p a o", p=128)
            o3 = b_wq8r[:, :].rearrange("p (a o) -> p a o", o=2048)
            t_wq8r_l = []
            for gidx in range(8):
                t_wq8r_l.append(sch.add(
                    ("Q", "R", "G")[gidx % 3],
                    (lambda g=gidx: (lambda e: e.dma_start(
                        o3[:, g * 8:(g + 1) * 8, :], s3[:, g * 8:(g + 1) * 8, :])))(),
                    dma=True))
            t_wq8r = t_wq8r_l

            t_x1cur = [None] * NT
            t_x1cur[0] = sch.add("G", lambda e: e.dma_start(c_x1[:, :], d_x1[0:128, :]), dma=True)

            t_k2e = {}
            t_qc = {}
            t_q4 = {}
            t_ktr2 = {}
            t_ev2 = [[None] * 4 for _ in range(NT)]
            t_mm2 = [[None] * 4 for _ in range(NT)]
            t_st = [None] * NT

            t_h2l = {}
            h2_state = {"n": 0}

            def pump_h2(upto):
                while h2_state["n"] < min(upto, NT * 16):
                    n = h2_state["n"]
                    h2_state["n"] = n + 1
                    w = [t_qc[n - NH2R]] if n >= NH2R else []
                    t_h2l[n] = sch.add("Q", (lambda n=n, b=n % NH2R: (lambda e: e.dma_start(
                        c_h2i[b][:, :], d_h2[(n // 16) * 128:(n // 16 + 1) * 128,
                                             (n % 16) * 512:(n % 16 + 1) * 512])))(),
                        w, dma=True)

            def build_quant(bt):
                for ch in range(16):
                    cg = bt * 16 + ch
                    pump_h2(cg + NH2R)
                    hv = c_h2i[cg % NH2R][:, :]

                    w = [t_h2l[cg], t_c23]
                    if cg >= 2:
                        w.append(t_q4[cg - 2])
                    t_q2 = sch.add("S", (lambda hv=hv, bt=bt, r=cg % 2: lambda e: e.activation(
                        c_kt[r][:, :], hv, ACTF.Copy, scale=v_c2[:, bt:bt + 1]))(), w)

                    w = [t_q2] + ([t_q4[cg - 2]] if cg >= 2 else [])
                    t_qc[cg] = _seq(sch, "V", [
                        (lambda hv=hv, bt=bt, r=cg % 2: lambda e: e.scalar_tensor_tensor(
                            c_m2[:, :], hv, v_c2[:, bt:bt + 1], c_kt[r][:, :],
                            ALU.mult, ALU.subtract))(),
                        (lambda r=cg % 2: lambda e: e.tensor_scalar(
                            c_m1[r][:, :], c_m2[:, :], -1.0, 0.5,
                            ALU.mult, ALU.is_gt))(),
                        (lambda r=cg % 2: lambda e: e.scalar_tensor_tensor(
                            c_m1[r][:, :], c_m2[:, :], 0.5,
                            c_m1[r][:, :], ALU.is_gt, ALU.subtract))(),
                    ], w)
                    w = [t_qc[cg]] + ([t_ktr2[cg - NKB]] if cg >= NKB else [])
                    t_q4[cg] = sch.add("V", (lambda kbb=cg % NKB, r=cg % 2: lambda e: e.tensor_add(
                        c_kb[kbb][:, :], c_kt[r][:, :], c_m1[r][:, :]))(), w)

            def build_transpose(bt, chs=range(16)):
                for ch in chs:
                    cg = bt * 16 + ch

                    def em_t2(e, kbb=cg % NKB, pb=cg % 2):
                        for j in range(4):
                            inst = e.transpose(ps_trb2[pb][:, j * 128:(j + 1) * 128],
                                               c_kb[kbb][:, j * 128:(j + 1) * 128],
                                               c_idb[:, :])
                        return inst
                    w = [t_q4[cg]]
                    if cg >= 2:
                        w.append(t_k2e[cg - 2])
                    t_ktr2[cg] = sch.add("T", em_t2, w)
                    t_k2e[cg] = sch.add("S", (lambda bt=bt, ch=ch, pb=cg % 2: (lambda e: e.copy(
                        k2T[bt % 2][:, ch * 512:(ch + 1) * 512], ps_trb2[pb][:, 0:512])))(),
                        [t_ktr2[cg]])

            build_quant(0)
            build_transpose(0)
            t_sq2o_prev = None

            for tt in range(NT):
                if tt + 1 < NT:
                    build_quant(tt + 1)

                def em_m2(e, tt=tt):
                    for ht in range(NH):
                        lhs = k2T[tt % 2][:, ht * 128:(ht + 1) * 128]
                        for oc in range(4):
                            inst = e.matmul(ps_o[(4 * tt + oc) % 6][:, :], lhs,
                                            b_wq8r[:, ht * 2048 + oc * 512:
                                                   ht * 2048 + (oc + 1) * 512],
                                            start=(ht == 0), stop=(ht == NH - 1))
                    return inst
                w = [t_k2e[tt * 16 + 15]]
                if tt == 0:
                    w.extend(t_wq8r)
                if tt >= 1:
                    w.append(t_ev2[tt - 1][0])
                    w.append(t_ev2[tt - 1][1])
                t_mmall = sch.add("T", em_m2, w)
                if tt + 1 < NT:
                    build_transpose(tt + 1)
                for oc in range(4):
                    t_mm2[tt][oc] = t_mmall

                    def em_z2(e, tt=tt, oc=oc):
                        return e.scalar_tensor_tensor(
                            c_z2[:, oc * 512:(oc + 1) * 512], ps_o[(4 * tt + oc) % 6][:, :],
                            v_c3[:, tt:tt + 1], c_x1[:, oc * 512:(oc + 1) * 512],
                            ALU.mult, ALU.add, accum_out=c_zs[:, oc:oc + 1])
                    w = [t_mmall, t_x1cur[tt]]
                    if tt >= 1 and oc == 0:
                        w.append(t_st[tt - 1])
                    t_ev2[tt][oc] = sch.add("V", em_z2, w)

                    t_sq2o = sch.add("S", (lambda oc=oc, tt=tt: (lambda e: e.activation(
                        k2T[tt % 2][:, 0:512], c_z2[:, oc * 512:(oc + 1) * 512], ACTF.Square,
                        accum_out=c_zs[:, 4 + oc:5 + oc])))(), [t_ev2[tt][oc]],
                        hard_waits=[t_sq2o_prev])
                    t_sq2o_prev = t_sq2o

                t_var2 = _seq(sch, "V", [
                    lambda e: e.reduce_sum(v_ti[:, 0:1], c_zs[:, 0:4], axis=AX.X),
                    lambda e: e.reduce_sum(v_ti[:, 1:2], c_zs[:, 4:8], axis=AX.X),
                    lambda e: e.tensor_scalar(v_ti[:, 2:3], v_ti[:, 0:1], INV_D, None, ALU.mult),
                    lambda e: e.tensor_scalar(v_ti[:, 3:4], v_ti[:, 1:2], INV_D, None, ALU.mult),
                    lambda e: e.tensor_mul(v_ti[:, 4:5], v_ti[:, 2:3], v_ti[:, 2:3]),
                    lambda e: e.tensor_sub(v_ti[:, 3:4], v_ti[:, 3:4], v_ti[:, 4:5]),
                ], [t_sq2o])
                t_sqr = sch.add("S", lambda e: e.activation(
                    v_ti[:, 5:6], v_ti[:, 3:4], ACTF.Sqrt, bias=c_eps[:, 0:1]), [t_var2])

                t_norm = _seq(sch, "V", [
                    lambda e: e.reciprocal(v_ti[:, 6:7], v_ti[:, 5:6]),
                    lambda e: e.tensor_scalar(c_z2[:, :], c_z2[:, :], v_ti[:, 2:3],
                                              v_ti[:, 6:7], ALU.subtract, ALU.mult),
                ], [t_sqr])
                t_st[tt] = sch.add("G", (lambda tt=tt: (lambda e: e.dma_start(
                    d_out[tt * 128:(tt + 1) * 128, :], c_z2[:, :])))(),
                    [t_norm], dma=True)

                if tt + 1 < NT:
                    t_x1cur[tt + 1] = sch.add("G", (lambda tt=tt: (lambda e: e.dma_start(
                        c_x1[:, :], d_x1[(tt + 1) * 128:(tt + 2) * 128, :])))(),
                        [t_ev2[tt][3]], dma=True)

            _wire(nc, sch, sems)

    return nc


_NC_CACHE = {}


def kernel(L0_attn_x, L0_attn_Wq, L0_attn_bq, L0_attn_Wkv, L0_attn_bkv,
           L0_ffn_norm_w, L0_ffn_w_gv, L0_ffn_w_out):
    x = np.ascontiguousarray(np.asarray(L0_attn_x, dtype=np.float32))
    wkv = np.asarray(L0_attn_Wkv, dtype=np.float32)
    bkv = np.asarray(L0_attn_bkv, dtype=np.float32)
    wgv = np.asarray(L0_ffn_w_gv, dtype=np.float32)
    wout = np.asarray(L0_ffn_w_out, dtype=np.float32)
    nw = np.asarray(L0_ffn_norm_w, dtype=np.float32)

    wgvT = np.ascontiguousarray(wgv.T)
    woutT = np.ascontiguousarray(wout.T)
    wvT = np.ascontiguousarray(wkv[128:256, :].T)
    bv = np.ascontiguousarray(bkv[128:256].reshape(1, 128))
    nwb = np.ascontiguousarray(np.broadcast_to(nw, (128, D)))
    idf = np.eye(128, dtype=np.float32)
    idb = np.eye(128).astype(ml_dtypes.bfloat16)
    ones1 = np.ones((1, 128), dtype=np.float32)
    ones128 = np.ones((128, 1), dtype=np.float32)

    if "nc" not in _NC_CACHE:
        _NC_CACHE["nc"] = build_nc()
    nc = _NC_CACHE["nc"]

    in_maps = []
    for c in range(8):
        in_maps.append({
            "x": np.ascontiguousarray(x[c]),
            "xT": np.ascontiguousarray(
                x[c].reshape(16, 128, 16, 128).transpose(0, 3, 2, 1).reshape(T, D)),
            "wgvT": wgvT, "woutT": woutT, "wvT": wvT, "bv": bv,
            "nwb": nwb, "idf": idf, "idb": idb,
            "ones1": ones1, "ones128": ones128,
            "epsv": np.full((128, 1), EPS, dtype=np.float32),
        })
    res = run_bass_kernel_spmd(nc, in_maps, core_ids=list(range(8)))
    out = np.stack([np.asarray(res.results[c]["out"]) for c in range(8)], axis=0)
    return out.astype(np.float32)
